# version 20
# baseline (speedup 1.0000x reference)
import numpy as np
import concourse.bass as bass
import concourse.mybir as mybir
from concourse.bass_utils import run_bass_kernel_spmd

F32 = mybir.dt.float32
BF16 = mybir.dt.bfloat16
I32 = mybir.dt.int32
AF = mybir.ActivationFunctionType
ALU = mybir.AluOpType
AX = mybir.AxisListType

D = 1024
S = 2048
NB = 2
NCORES = 8
NCH = 8
HID = 4096
NHC = 32
TA = 512
TB = 256
MOE_TB = 512
NEXP = 8
HEAD = 64
LADDER = [128.0, 64.0, 32.0, 16.0, 8.0, 4.0, 2.0, 1.0]
C_EPS, C_NEG1, C_NEGHALF, C_1702, C_M20 = 0, 1, 2, 3, 4


class Prog:
    ENGS = ("te", "ve", "ac", "gp", "sy")

    def __init__(self):
        self.ins = {e: [] for e in self.ENGS}
        self.last_w = {}
        self.last_r = {}
        self.barrier_fences = []

    def emit(self, eng, fn, r=(), w=(), dma=False):
        idx = len(self.ins[eng])
        waits = set()
        for key in self.barrier_fences:
            if key[0] != eng:
                waits.add(key)
        for key in r:
            lw = self.last_w.get(key)
            if lw is not None:
                waits.add(lw)
        for key in w:
            for rr in self.last_r.get(key, ()):
                if rr[0] != eng:
                    waits.add(rr)
            lw = self.last_w.get(key)
            if lw is not None and lw[0] != eng:
                waits.add(lw)
        fam = w[0] if (dma and w) else None
        self.ins[eng].append({"fn": fn, "waits": waits, "inc": dma, "dma": dma,
                              "fam": fam})
        for key in w:
            self.last_w[key] = (eng, idx)
            self.last_r[key] = []
        for key in r:
            self.last_r.setdefault(key, []).append((eng, idx))

    def barrier(self):
        self.barrier_fences = [(e, len(self.ins[e]) - 1)
                               for e in self.ENGS if self.ins[e]]


def build_program():
    from contextlib import ExitStack
    nc = bass.Bass("TRN2", target_bir_lowering=False)
    P = Prog()

    def din(name, shape, dt=F32):
        return nc.dram_tensor(name, shape, dt, kind="ExternalInput").ap()

    x_fm = din("x_fm", [NB, D, S])
    xb0_fm = din("xb0_fm", [D, S])
    eid_i = din("eid", [2, 1])
    consts = din("consts", [128, 8])
    ident = din("ident", [128, 128])
    onesc = din("onesc", [1, 128])
    ones128 = din("ones128", [128, 1])
    mask1 = din("mask1", [128, 128])
    maskT = din("maskT", [128, 128])
    mask_moe = din("mask_moe", [128, 1])
    idxs = din("idxs", [2, S], I32)
    rws = din("rws", [1, S])
    W1T_i = din("W1T", [D, D])
    b1c_i = din("b1c", [128, NCH])
    w2t_i = din("w2t", [D, 16])
    w3t_i = din("w3t", [D, 16])
    b23_i = din("b23", [16, 2])
    gupT = din("gupT", [D, 2048])
    gubc = din("gubc", [128, 16])
    dwnT = din("dwnT", [D, D])
    dbrow = din("dbrow", [1, D])
    wq_i = din("wq", [D, HEAD])
    wk_i = din("wk", [D, HEAD])
    wv_i = din("wv", [D, HEAD])
    bqkv_i = din("bqkv", [HEAD, 3])
    wo_i = din("wo", [HEAD, D])
    boc_i = din("boc", [128, NCH])
    w1T_in = din("w1T_in", [D, HID])
    w2T_in = din("w2T_in", [HID, D])
    cwc_i = din("cwc", [3, 128, NHC])
    cbc_i = din("cbc", [128, NHC])

    out_fm = nc.dram_tensor("out_fm", [NB, D, S], F32, kind="ExternalOutput").ap()

    ag_in = nc.dram_tensor("ag_in", [D, S], F32).ap()
    ag_out = nc.dram_tensor("ag_out", [D, S], F32).ap()
    x1_h = nc.dram_tensor("x1_h", [NB, D, S], F32).ap()
    s1_h = nc.dram_tensor("s1_h", [NB, D, S], F32).ap()
    h_h = nc.dram_tensor("h_h", [NB, HID, S], F32).ap()
    qkv_h = nc.dram_tensor("qkv_h", [3, NB, HEAD, S], F32).ap()
    xq_h = nc.dram_tensor("xq_h", [NB, D, S], BF16).ap()

    es_top = ExitStack()

    def sb(name, shape, dt=F32, es=None):
        return (es or es_top).enter_context(nc.sbuf_tensor(name, shape, dt))

    c_c = sb("c_c", [128, 8])
    c_id = sb("c_id", [128, 128])
    c_o1 = sb("c_o1", [1, 128])
    c_o128 = sb("c_o128", [128, 1])
    c_m1 = sb("c_m1", [128, 128])
    c_mT = sb("c_mT", [128, 128])
    c_mmoe = sb("c_mmoe", [128, 1])
    m2tok = sb("m2tok", [128, NB * 16])
    i2tok = sb("i2tok", [128, NB * 16])
    s1tok = sb("s1tok", [128, NB * 16])
    r1tok = sb("r1tok", [128, NB * 16])
    r2tok = sb("r2tok", [128, NB * 16])
    s2tok = sb("s2tok", [128, NB * 16])
    rowm = sb("rowm", [1, TA])
    rowi = sb("rowi", [1, TA])
    rowt = sb("rowt", [1, TA])
    rowt2 = sb("rowt2", [1, TA])
    val1 = sb("val1", [1, 4])
    wsc1 = sb("wsc1", [128, 2])
    rwsc = sb("rwsc", [1, 2])
    s_w2t = sb("s_w2t", [128, NCH * 16])
    s_w3t = sb("s_w3t", [128, NCH * 16])
    s_b23 = sb("s_b23", [16, 2])
    s_b1c = sb("s_b1c", [128, NCH])
    s_wq = sb("s_wq", [128, NCH * HEAD])
    s_wk = sb("s_wk", [128, NCH * HEAD])
    s_wv = sb("s_wv", [128, NCH * HEAD])
    s_bqkv = sb("s_bqkv", [HEAD, 3])
    s_wo = sb("s_wo", [HEAD, D])
    s_boc = sb("s_boc", [128, NCH])
    s_cw0 = sb("s_cw0", [128, NHC])
    s_cw1 = sb("s_cw1", [128, NHC])
    s_cw2 = sb("s_cw2", [128, NHC])
    s_cb = sb("s_cb", [128, NHC])

    pp = [es_top.enter_context(nc.psum_tensor(f"pp{i}", [128, 512], F32))
          for i in range(8)]
    pr = [pp[6], pp[7]]

    def ldma(dst, src, r=(), w=(), q="sy"):
        P.emit(q, lambda e, d=dst, s=src: e.dma_start(out=d, in_=s), r=r, w=w, dma=True)

    def sdma(dst, src, r=(), w=(), q="gp"):
        P.emit(q, lambda e, d=dst, s=src: e.dma_start(out=d, in_=s), r=r, w=w, dma=True)

    def mm(out, lhsT, rhs, start, stop, r=(), w=()):
        P.emit("te", lambda e, o=out, l=lhsT, x=rhs, st=start, sp=stop:
               e.matmul(o, l, x, start=st, stop=sp), r=r, w=w)

    def transp(out, in_, r=(), w=(), npart=128):
        P.emit("te", lambda e, o=out, i=in_, n=npart:
               e.transpose(o, i, c_id[0:n, 0:n]),
               r=tuple(r) + ("c_id",), w=w)

    def act(out, in_, func, bias=0.0, scale=1.0, r=(), w=()):
        P.emit("ac", lambda e, o=out, i=in_, f=func, b=bias, s=scale:
               e.activation(o, i, f, bias=b, scale=s), r=r, w=w)

    def tt(out, a, b, op, r=(), w=()):
        P.emit("ve", lambda e, o=out, x=a, y=b, p=op: e.tensor_tensor(o, x, y, p),
               r=r, w=w)

    def ts(out, in_, s1, s2, op0, op1, r=(), w=()):
        P.emit("ve", lambda e, o=out, i=in_, a=s1, b=s2, p0=op0, p1=op1:
               e.tensor_scalar(o, i, a, b, p0, p1), r=r, w=w)

    def ts1(out, in_, s1, op, r=(), w=()):
        P.emit("ve", lambda e, o=out, i=in_, a=s1, p=op:
               e.tensor_single_scalar(o, i, a, p), r=r, w=w)

    def tsm(out, in_, sap, r=(), w=()):
        P.emit("ve", lambda e, o=out, i=in_, a=sap: e.tensor_scalar_mul(o, i, a),
               r=r, w=w)

    def stt(out, in0, scalar, in1, op0, op1, r=(), w=()):
        P.emit("ve", lambda e, o=out, i0=in0, sc=scalar, i1=in1, p0=op0, p1=op1:
               e.scalar_tensor_tensor(o, i0, sc, i1, p0, p1), r=r, w=w)

    def vcopy(out, in_, r=(), w=()):
        P.emit("ve", lambda e, o=out, i=in_: e.tensor_copy(o, i), r=r, w=w)

    def vred(out, in_, op, absval=False, r=(), w=()):
        P.emit("ve", lambda e, o=out, i=in_, p=op, av=absval:
               e.tensor_reduce(o, i, AX.X, p, apply_absolute_value=av), r=r, w=w)

    def vrecip(out, in_, r=(), w=()):
        P.emit("ve", lambda e, o=out, i=in_: e.reciprocal(o, i), r=r, w=w)

    def vmemset(ap_, val, r=(), w=()):
        P.emit("ve", lambda e, o=ap_, v=val: e.memset(o, v), r=r, w=w)

    ldma(c_c[:, :], consts[:, :], w=("c_c",))
    ldma(c_id[:, :], ident[:, :], w=("c_id",))
    ldma(c_o1[:, :], onesc[:, :], w=("c_o1",))
    ldma(c_o128[:, :], ones128[:, :], w=("c_o128",))
    ldma(c_m1[:, :], mask1[:, :], w=("c_m1",))
    ldma(c_mT[:, :], maskT[:, :], w=("c_mT",))
    ldma(c_mmoe[:, :], mask_moe[:, :], w=("c_mmoe",))
    for c in range(NCH):
        ldma(s_w2t[:, c * 16:(c + 1) * 16], w2t_i[c * 128:(c + 1) * 128, :], w=("s_w2t",))
        ldma(s_w3t[:, c * 16:(c + 1) * 16], w3t_i[c * 128:(c + 1) * 128, :], w=("s_w3t",))
        ldma(s_wq[:, c * HEAD:(c + 1) * HEAD], wq_i[c * 128:(c + 1) * 128, :], w=("s_wq",))
        ldma(s_wk[:, c * HEAD:(c + 1) * HEAD], wk_i[c * 128:(c + 1) * 128, :], w=("s_wk",))
        ldma(s_wv[:, c * HEAD:(c + 1) * HEAD], wv_i[c * 128:(c + 1) * 128, :], w=("s_wv",))
    ldma(s_b23[:, :], b23_i[:, :], w=("s_b23",))
    ldma(s_b1c[:, :], b1c_i[:, :], w=("s_b1c",))
    ldma(s_bqkv[:, :], bqkv_i[:, :], w=("s_bqkv",))
    ldma(s_wo[:, :], wo_i[:, :], w=("s_wo",))
    ldma(s_boc[:, :], boc_i[:, :], w=("s_boc",))
    ldma(s_cw0[:, :], cwc_i[0, :, :], w=("s_cw0",))
    ldma(s_cw1[:, :], cwc_i[1, :, :], w=("s_cw1",))
    ldma(s_cw2[:, :], cwc_i[2, :, :], w=("s_cw2",))
    ldma(s_cb[:, :], cbc_i[:, :], w=("s_cb",))

    def emit_ln(x_t, xkey, T, xn_t, xnkey, sq_t, sqkey):
        for c in range(NCH):
            mm(pr[0][:1, :T], c_o128[:, :1], x_t[:, c * T:(c + 1) * T],
               start=(c == 0), stop=(c == NCH - 1), r=(xkey, "c_o128"), w=("ps6",))
        for c in range(NCH):
            k = c % 2
            act(sq_t[:, k * T:(k + 1) * T], x_t[:, c * T:(c + 1) * T], AF.Square,
                r=(xkey,), w=(sqkey + str(k),))
            mm(pr[1][:1, :T], c_o128[:, :1], sq_t[:, k * T:(k + 1) * T],
               start=(c == 0), stop=(c == NCH - 1),
               r=(sqkey + str(k), "c_o128"), w=("ps7",))
        ts1(rowm[:1, :T], pr[0][:1, :T], 1.0 / D, ALU.mult, r=("ps6",), w=("rowm",))
        tt(rowt2[:1, :T], rowm[:1, :T], rowm[:1, :T], ALU.mult, r=("rowm",), w=("rowt2",))
        stt(rowt2[:1, :T], pr[1][:1, :T], 1.0 / D, rowt2[:1, :T],
            ALU.mult, ALU.subtract, r=("ps7", "rowt2"), w=("rowt2",))
        act(rowt2[:1, :T], rowt2[:1, :T], AF.Ln, bias=c_c[:1, C_EPS:C_EPS + 1],
            r=("rowt2", "c_c"), w=("rowt2",))
        act(rowi[:1, :T], rowt2[:1, :T], AF.Exp,
            scale=c_c[:1, C_NEGHALF:C_NEGHALF + 1], r=("rowt2", "c_c"), w=("rowi",))
        mm(pp[0][:, :T], c_o1[:1, :], rowm[:1, :T], start=True, stop=True,
           r=("rowm", "c_o1"), w=("ps0",))
        mm(pp[1][:, :T], c_o1[:1, :], rowi[:1, :T], start=True, stop=True,
           r=("rowi", "c_o1"), w=("ps1",))
        for c in range(NCH):
            tt(xn_t[:, c * T:(c + 1) * T], x_t[:, c * T:(c + 1) * T], pp[0][:, :T],
               ALU.subtract, r=(xkey, "ps0"), w=(xnkey,))
            tt(xn_t[:, c * T:(c + 1) * T], xn_t[:, c * T:(c + 1) * T], pp[1][:, :T],
               ALU.mult, r=(xnkey, "ps1"), w=(xnkey,))

    def row_to_tok(row_t, rkey, T, dst, dkey, col0):
        for sub in range(T // 128):
            transp(pp[2][:, 0:1], row_t[:1, sub * 128:(sub + 1) * 128],
                   r=(rkey,), w=("ps2",), npart=1)
            vcopy(dst[:, col0 + sub:col0 + sub + 1], pp[2][:, 0:1],
                  r=("ps2",), w=(dkey,))

    def build_bcast(toksrc, tkey, b, lo, width, loff, dst, dkey, dstw, rwcol=None):
        if loff > 0 or loff + width < dstw:
            vmemset(dst[:, :dstw], 1.0, w=(dkey,))
        tl0 = lo // 128
        ntl = (lo + width - 1) // 128 - tl0 + 1
        for j in range(ntl):
            transp(pp[2][0:1, 0:128],
                   toksrc[:, b * 16 + tl0 + j: b * 16 + tl0 + j + 1],
                   r=(tkey,), w=("ps2",))
            vcopy(rowt[:1, j * 128:(j + 1) * 128], pp[2][0:1, 0:128],
                  r=("ps2",), w=("rowt",))
        off = lo - tl0 * 128
        if rwcol is not None:
            P.emit("ve", lambda e, o=rowt[:1, off:off + width],
                   i=rowt[:1, off:off + width], a=rwsc[:1, rwcol:rwcol + 1]:
                   e.tensor_scalar_mul(o, i, a), r=("rowt", "rwsc"), w=("rowt",))
        mm(pp[3][:, :width], c_o1[:1, :], rowt[:1, off:off + width],
           start=True, stop=True, r=("rowt", "c_o1"), w=("ps3",))
        vcopy(dst[:, loff:loff + width], pp[3][:, :width], r=("ps3",), w=(dkey,))

    def ladder(u0_ap, u_ap, kt_ap, u0key, ukey, ktkey):
        ts1(u_ap, u0_ap, 0.0, ALU.add, r=(u0key,), w=(ukey,))
        for kv in LADDER:
            ts(kt_ap, u_ap, kv, kv, ALU.is_ge, ALU.mult, r=(ukey,), w=(ktkey,))
            tt(u_ap, u_ap, kt_ap, ALU.subtract, r=(ukey, ktkey), w=(ukey,))

    es = ExitStack()
    MB = 512
    m_gup = sb("m_gup", [128, NCH * 2048], es=es)
    m_dwn = sb("m_dwn", [128, NCH * 1024], es=es)
    m_x0 = sb("m_x0", [128, NCH * MB], es=es)
    m_xn0 = sb("m_xn0", [128, NCH * MB], es=es)
    m_int = sb("m_int", [128, NCH * MB], es=es)
    m_sq = sb("m_sq", [128, 2 * MB], es=es)
    m_g = sb("m_g", [128, MB], es=es)
    m_u = sb("m_u", [128, MB], es=es)
    m_do = [sb(f"m_do{i}", [128, MB], es=es) for i in range(2)]
    m_rw = sb("m_rw", [1, MB], es=es)
    m_coef = sb("m_coef", [1, MB], es=es)
    m_c2 = sb("m_c2", [1, MB], es=es)
    m_gub = sb("m_gub", [128, 16], es=es)
    m_dbr = sb("m_dbr", [1, D], es=es)
    m_eid = sb("m_eid", [2, 1], es=es)
    m_idx = sb("m_idx", [1, 2 * MB], I32, es=es)
    g_idxf = sb("g_idxf", [1, 2 * MB], es=es)
    g_d = sb("g_d", [1, 2 * MB], es=es)
    g_t = sb("g_t", [1, 2 * MB], es=es)

    for c in range(NCH):
        ldma(m_gup[:, c * 2048:(c + 1) * 2048],
             gupT[c * 128:(c + 1) * 128, :], w=("m_gup",))
        ldma(m_dwn[:, c * 1024:(c + 1) * 1024],
             dwnT[c * 128:(c + 1) * 128, :], w=("m_dwn",))
    ldma(m_gub[:, :], gubc[:, :], w=("m_gub",))
    ldma(m_dbr[:1, :], dbrow[:, :], w=("m_dbr",))
    ldma(m_eid[0:2, :], eid_i[:, :], w=("m_eid",))
    for ti in range(S // MB):
        t0 = ti * MB
        for c in range(NCH):
            ldma(m_x0[:, c * MB:(c + 1) * MB],
                 xb0_fm[c * 128:(c + 1) * 128, t0:t0 + MB], w=("m_x0",))
        ldma(m_idx[0:1, 0:MB], idxs[0:1, t0:t0 + MB], w=("m_idx",))
        ldma(m_idx[0:1, MB:2 * MB], idxs[1:2, t0:t0 + MB], w=("m_idx",))
        ldma(m_rw[:1, :], rws[:1, t0:t0 + MB], w=("m_rw",))
        vcopy(g_idxf[0:1, :], m_idx[0:1, :], r=("m_idx",), w=("g_idxf",))
        emit_ln(m_x0, "m_x0", MB, m_xn0, "m_xn0", m_sq, "m_sq")
        act(g_d[0:1, :], g_idxf[0:1, :], AF.Identity, bias=m_eid[0:1, 0:1],
            r=("g_idxf", "m_eid"), w=("g_d",))
        ts1(g_t[0:1, :], g_d[0:1, :], -0.5, ALU.is_ge, r=("g_d",), w=("g_t",))
        stt(g_t[0:1, :], g_d[0:1, :], 0.5, g_t[0:1, :], ALU.is_le, ALU.mult,
            r=("g_d", "g_t"), w=("g_t",))
        tt(m_coef[:1, :], g_t[0:1, 0:MB], g_t[0:1, MB:2 * MB], ALU.add,
           r=("g_t",), w=("m_coef",))
        tt(m_coef[:1, :], m_coef[:1, :], m_rw[:1, :], ALU.mult,
           r=("m_coef", "m_rw"), w=("m_coef",))
        ts1(m_coef[:1, :], m_coef[:1, :], 1.0 / 1.702, ALU.mult,
            r=("m_coef",), w=("m_coef",))
        mm(pp[4][:, :MB], c_o1[:1, :], m_coef[:1, :],
           start=True, stop=True, r=("m_coef", "c_o1"), w=("ps4",))
        gb = [2, 5, 6]
        for g in range(NCH):
            for (pi, oc, dst, dkey) in ((0, g, m_g, "m_g"),
                                        (1, g + NCH, m_u, "m_u")):
                bank = gb[(2 * g + pi) % 3]
                for c in range(NCH):
                    mm(pp[bank][:, :MB],
                       m_gup[:, c * 2048 + oc * 128: c * 2048 + oc * 128 + 128],
                       m_xn0[:, c * MB:(c + 1) * MB],
                       start=(c == 0), stop=(c == NCH - 1),
                       r=("m_gup", "m_xn0"), w=(f"ps{bank}",))
                act(dst[:, :MB], pp[bank][:, :MB], AF.Identity,
                    bias=m_gub[:, oc:oc + 1],
                    r=(f"ps{bank}", "m_gub"), w=(dkey,))
            ts1(m_g[:, :MB], m_g[:, :MB], 7.0, ALU.min,
                r=("m_g",), w=("m_g",))
            act(m_g[:, :MB], m_g[:, :MB], AF.Silu,
                scale=c_c[:, C_1702:C_1702 + 1], r=("m_g", "c_c"), w=("m_g",))
            ts(m_u[:, :MB], m_u[:, :MB], -7.0, 7.0, ALU.max, ALU.min,
               r=("m_u",), w=("m_u",))
            stt(m_int[:, g * MB:(g + 1) * MB], m_u[:, :MB], 1.0,
                m_g[:, :MB], ALU.add, ALU.mult, r=("m_u", "m_g"), w=("m_int",))
            tt(m_int[:, g * MB:(g + 1) * MB],
               m_int[:, g * MB:(g + 1) * MB], pp[4][:, :MB],
               ALU.mult, r=("m_int", "ps4"), w=("m_int",))
        for oc in range(NCH):
            bank = 3 if oc % 2 == 0 else 7
            for c in range(NCH):
                mm(pp[bank][:, :MB],
                   m_dwn[:, c * 1024 + oc * 128: c * 1024 + oc * 128 + 128],
                   m_int[:, c * MB:(c + 1) * MB],
                   start=(c == 0), stop=False,
                   r=("m_dwn", "m_int"), w=(f"ps{bank}",))
            mm(pp[bank][:, :MB], m_dbr[:1, oc * 128:(oc + 1) * 128],
               m_coef[:1, :], start=False, stop=True,
               r=("m_dbr", "m_coef"), w=(f"ps{bank}",))
            dk = f"m_do{oc % 2}"
            vcopy(m_do[oc % 2][:, :], pp[bank][:, :MB], r=(f"ps{bank}",), w=(dk,))
            sdma(ag_in[oc * 128:(oc + 1) * 128, t0:t0 + MB], m_do[oc % 2][:, :],
                 r=(dk,), w=("agin",))
    P.emit("gp", lambda e: e.collective_compute(
        "AllReduce", ALU.add, replica_groups=[list(range(NCORES))],
        ins=[ag_in[:, :].opt()], outs=[ag_out[:, :].opt()]),
        r=("agin",), w=("agout",))
    es.close()
    P.barrier()

    es = ExitStack()
    a_x = sb("a_x2", [128, NCH * TA], es=es)
    a_xn = sb("a_xn2", [128, NCH * TA], es=es)
    a_sq = sb("a_sq2", [128, 2 * TA], es=es)
    a_W1 = sb("a_W1", [128, NCH * D], es=es)
    a_bc = sb("a_bc", [16, TA], es=es)
    a_cc = sb("a_cc", [16, TA], es=es)
    a_dl = sb("a_dl", [128, TA], es=es)
    a_t1 = sb("a_t1", [128, TA], es=es)
    a_moe = sb("a_moe", [128, TA], es=es)
    a_fmax = sb("a_fmax", [128, TA], es=es)
    a_fmin = sb("a_fmin", [128, TA], es=es)
    a_mx = sb("a_mx", [128, 2], es=es)
    a_qkv = sb("a_qkv", [HEAD, TA], es=es)
    a_u0 = sb("a_u0", [128, NCH * TA], es=es)
    a_i32 = sb("a_i32", [128, NCH * TA], I32, es=es)
    a_xqt = sb("a_xqt", [128, NCH * TA], BF16, es=es)

    for c in range(NCH):
        ldma(a_W1[:, c * D:(c + 1) * D], W1T_i[c * 128:(c + 1) * 128, :], w=("a_W1",))

    for b in range(NB):
        for ti in range(S // TA):
            t0 = ti * TA
            for c in range(NCH):
                ldma(a_x[:, c * TA:(c + 1) * TA],
                     x_fm[b, c * 128:(c + 1) * 128, t0:t0 + TA], w=("a_x",))
            emit_ln(a_x, "a_x", TA, a_xn, "a_xn", a_sq, "a_sq")
            for c in range(NCH):
                mm(pp[2][0:16, :TA], s_w2t[:, c * 16:(c + 1) * 16],
                   a_xn[:, c * TA:(c + 1) * TA], start=(c == 0), stop=(c == NCH - 1),
                   r=("s_w2t", "a_xn"), w=("ps2",))
                mm(pp[3][0:16, :TA], s_w3t[:, c * 16:(c + 1) * 16],
                   a_xn[:, c * TA:(c + 1) * TA], start=(c == 0), stop=(c == NCH - 1),
                   r=("s_w3t", "a_xn"), w=("ps3",))
            act(a_bc[0:16, :], pp[2][0:16, :TA], AF.Identity, bias=s_b23[:, 0:1],
                r=("ps2", "s_b23"), w=("a_bc",))
            act(a_cc[0:16, :], pp[3][0:16, :TA], AF.Identity, bias=s_b23[:, 1:2],
                r=("ps3", "s_b23"), w=("a_cc",))
            tt(a_bc[0:16, :], a_bc[0:16, :], a_cc[0:16, :], ALU.mult,
               r=("a_bc", "a_cc"), w=("a_bc",))
            mm(pr[0][:1, :TA], c_o128[0:16, :1], a_bc[0:16, :], start=True, stop=True,
               r=("a_bc", "c_o128"), w=("ps6",))
            vcopy(rowt[:1, :TA], pr[0][:1, :TA], r=("ps6",), w=("rowt",))
            mm(pp[4][:, :TA], c_o1[:1, :], rowt[:1, :TA], start=True, stop=True,
               r=("rowt", "c_o1"), w=("ps4",))
            for oc in range(NCH):
                bank = 5 if oc % 2 == 0 else 3
                for c in range(NCH):
                    mm(pp[bank][:, :TA], a_W1[:, c * D + oc * 128: c * D + oc * 128 + 128],
                       a_xn[:, c * TA:(c + 1) * TA], start=(c == 0),
                       stop=(c == NCH - 1), r=("a_W1", "a_xn"), w=(f"ps{bank}",))
                act(a_t1[:, :], pp[bank][:, :TA], AF.Abs, bias=s_b1c[:, oc:oc + 1],
                    r=(f"ps{bank}", "s_b1c"), w=("a_t1",))
                act(a_t1[:, :], a_t1[:, :], AF.Exp, scale=c_c[:, C_NEG1:C_NEG1 + 1],
                    r=("a_t1", "c_c"), w=("a_t1",))
                act(a_t1[:, :], a_t1[:, :], AF.Ln, bias=1.0, r=("a_t1",), w=("a_t1",))
                act(a_dl[:, :], pp[bank][:, :TA], AF.Relu, bias=s_b1c[:, oc:oc + 1],
                    r=(f"ps{bank}", "s_b1c"), w=("a_dl",))
                tt(a_dl[:, :], a_dl[:, :], a_t1[:, :], ALU.add,
                   r=("a_dl", "a_t1"), w=("a_dl",))
                tt(a_dl[:, :], a_dl[:, :], a_xn[:, oc * TA:(oc + 1) * TA], ALU.mult,
                   r=("a_dl", "a_xn"), w=("a_dl",))
                tt(a_dl[:, :], a_dl[:, :], pp[4][:, :TA], ALU.mult,
                   r=("a_dl", "ps4"), w=("a_dl",))
                tt(a_x[:, oc * TA:(oc + 1) * TA], a_x[:, oc * TA:(oc + 1) * TA],
                   a_dl[:, :], ALU.add, r=("a_x", "a_dl"), w=("a_x",))
                if b == 0:
                    ldma(a_moe[:, :TA],
                         ag_out[oc * 128:(oc + 1) * 128, t0:t0 + TA],
                         r=("agout",), w=("a_moe",))
                    stt(a_x[:, oc * TA:(oc + 1) * TA], a_moe[:, :], c_mmoe[:, :1],
                        a_x[:, oc * TA:(oc + 1) * TA], ALU.mult, ALU.add,
                        r=("a_moe", "c_mmoe", "a_x"), w=("a_x",))
                sdma(x1_h[b, oc * 128:(oc + 1) * 128, t0:t0 + TA],
                     a_x[:, oc * TA:(oc + 1) * TA], r=("a_x",), w=("x1h",))
            emit_ln(a_x, "a_x", TA, a_xn, "a_xn", a_sq, "a_sq")
            row_to_tok(rowm, "rowm", TA, m2tok, "m2tok", b * 16 + ti * (TA // 128))
            row_to_tok(rowi, "rowi", TA, i2tok, "i2tok", b * 16 + ti * (TA // 128))
            vcopy(a_fmax[:, :], a_xn[:, 0:TA], r=("a_xn",), w=("a_fmax",))
            vcopy(a_fmin[:, :], a_xn[:, 0:TA], r=("a_xn",), w=("a_fmin",))
            for c in range(1, NCH):
                tt(a_fmax[:, :], a_fmax[:, :], a_xn[:, c * TA:(c + 1) * TA], ALU.max,
                   r=("a_fmax", "a_xn"), w=("a_fmax",))
                tt(a_fmin[:, :], a_fmin[:, :], a_xn[:, c * TA:(c + 1) * TA], ALU.min,
                   r=("a_fmin", "a_xn"), w=("a_fmin",))
            ts1(a_fmin[:, :], a_fmin[:, :], -1.0, ALU.mult, r=("a_fmin",), w=("a_fmin",))
            tt(a_fmax[:, :], a_fmax[:, :], a_fmin[:, :], ALU.max,
               r=("a_fmax", "a_fmin"), w=("a_fmax",))
            for sub in range(TA // 128):
                col = b * 16 + ti * (TA // 128) + sub
                transp(pp[2][:, 0:128], a_fmax[:, sub * 128:(sub + 1) * 128],
                       r=("a_fmax",), w=("ps2",))
                vred(a_mx[:, 0:1], pp[2][:, 0:128], ALU.max, r=("ps2",), w=("a_mx",))
                ts1(a_mx[:, 0:1], a_mx[:, 0:1], 1e-5, ALU.max, r=("a_mx",), w=("a_mx",))
                vrecip(a_mx[:, 1:2], a_mx[:, 0:1], r=("a_mx",), w=("a_mx",))
                ts1(s1tok[:, col:col + 1], a_mx[:, 1:2], 127.0, ALU.mult,
                    r=("a_mx",), w=("s1tok",))
                vrecip(r1tok[:, col:col + 1], s1tok[:, col:col + 1],
                       r=("s1tok",), w=("r1tok",))
                transp(pp[2][0:1, 0:128], s1tok[:, col:col + 1],
                       r=("s1tok",), w=("ps2",))
                vcopy(rowt2[:1, sub * 128:(sub + 1) * 128], pp[2][0:1, 0:128],
                      r=("ps2",), w=("rowt2",))
            mm(pp[3][:, :TA], c_o1[:1, :], rowt2[:1, :TA], start=True, stop=True,
               r=("rowt2", "c_o1"), w=("ps3",))
            ts1(a_dl[:, :TA], pp[3][:, :TA], -1.0, ALU.mult, r=("ps3",), w=("a_dl",))
            tt(a_u0[:, :], a_xn[:, :],
               bass.AP(a_dl, 0, [[TA, 128], [0, NCH], [1, TA]]), ALU.mult,
               r=("a_xn", "a_dl"), w=("a_u0",))
            for hp in range(2):
                hw = NCH * TA // 2
                act(a_i32[:, hp * hw:(hp + 1) * hw], a_u0[:, hp * hw:(hp + 1) * hw],
                    AF.Identity, r=("a_u0",), w=("a_i32",))
                vcopy(a_xqt[:, hp * hw:(hp + 1) * hw], a_i32[:, hp * hw:(hp + 1) * hw],
                      r=("a_i32",), w=("a_xqt",))
            for c in range(NCH):
                sdma(xq_h[b, c * 128:(c + 1) * 128, t0:t0 + TA],
                     a_xqt[:, c * TA:(c + 1) * TA], r=("a_xqt",), w=("xqh",))
            for (wt, wkey, bi) in ((s_wq, "s_wq", 0), (s_wk, "s_wk", 1),
                                   (s_wv, "s_wv", 2)):
                bank = 2 if bi % 2 == 0 else 5
                for c in range(NCH):
                    mm(pp[bank][0:HEAD, :TA], wt[:, c * HEAD:(c + 1) * HEAD],
                       a_xn[:, c * TA:(c + 1) * TA], start=(c == 0),
                       stop=(c == NCH - 1), r=(wkey, "a_xn"), w=(f"ps{bank}",))
                act(a_qkv[0:HEAD, :TA], pp[bank][0:HEAD, :TA],
                    AF.Identity, bias=s_bqkv[:, bi:bi + 1],
                    r=(f"ps{bank}", "s_bqkv"), w=("a_qkv",))
                sdma(qkv_h[bi, b, 0:HEAD, t0:t0 + TA], a_qkv[0:HEAD, :TA],
                     r=("a_qkv",), w=("qkvh",))
    es.close()
    P.barrier()

    es_w1 = ExitStack()
    w1q = sb("w1q", [128, NCH * HID], BF16, es=es_w1)
    es = ExitStack()
    q_wf = sb("q_wf", [128, HID], es=es)
    q_wg = sb("q_wg", [128, HID], es=es)
    qw2 = [(q_wf, "q_wf"), (q_wg, "q_wg")]
    q_v = sb("q_v", [128, HID], es=es)
    q_ge = sb("q_ge", [128, HID], es=es)
    q_r = sb("q_r", [128, 3], es=es)

    def quant_weight(src_dram, nchunks, chunk_w, dstq, dstqkey, wcol):
        for c in range(nchunks):
            qb, qk = qw2[c % 2]
            ldma(qb[:, :chunk_w], src_dram[c * 128:(c + 1) * 128, :],
                 w=(qk,), q="sy" if c % 2 == 0 else "gp")
            vred(q_r[:, 0:1], qb[:, :chunk_w], ALU.add, absval=True,
                 r=(qk,), w=("q_r",))
            if c == 0:
                vcopy(q_r[:, 2:3], q_r[:, 0:1], r=("q_r",), w=("q_r",))
            else:
                tt(q_r[:, 2:3], q_r[:, 2:3], q_r[:, 0:1], ALU.add,
                   r=("q_r",), w=("q_r",))
        mm(pr[0][:1, 0:1], c_o128[:, :1], q_r[:, 2:3], start=True, stop=True,
           r=("q_r", "c_o128"), w=("ps6",))
        ts(val1[:1, 0:1], pr[0][:1, 0:1], 1.0 / (nchunks * 128 * chunk_w), 1e-5,
           ALU.mult, ALU.max, r=("ps6",), w=("val1",))
        vcopy(rwsc[:1, wcol:wcol + 1], val1[:1, 0:1], r=("val1",), w=("rwsc",))
        vrecip(val1[:1, 1:2], val1[:1, 0:1], r=("val1",), w=("val1",))
        mm(pp[2][:, 0:1], c_o1[:1, :], val1[:1, 1:2], start=True, stop=True,
           r=("val1", "c_o1"), w=("ps2",))
        vcopy(wsc1[:, wcol:wcol + 1], pp[2][:, 0:1], r=("ps2",), w=("wsc1",))
        for c in range(nchunks):
            qb, qk = qw2[c % 2]
            ldma(qb[:, :chunk_w], src_dram[c * 128:(c + 1) * 128, :],
                 w=(qk,), q="sy" if c % 2 == 0 else "gp")
            tsm(q_v[:, :chunk_w], qb[:, :chunk_w], wsc1[:, wcol:wcol + 1],
                r=(qk, "wsc1"), w=("q_v",))
            ts1(q_ge[:, :chunk_w], q_v[:, :chunk_w], 0.5, ALU.is_ge,
                r=("q_v",), w=("q_ge",))
            stt(q_ge[:, :chunk_w], q_v[:, :chunk_w], -0.5, q_ge[:, :chunk_w],
                ALU.is_le, ALU.add, r=("q_v", "q_ge"), w=("q_ge",))
            act(q_v[:, :chunk_w], qb[:, :chunk_w], AF.Sign,
                scale=c_c[:, C_NEG1:C_NEG1 + 1], r=(qk, "c_c"), w=("q_v",))
            tt(dstq[:, c * chunk_w:(c + 1) * chunk_w], q_v[:, :chunk_w],
               q_ge[:, :chunk_w], ALU.mult, r=("q_v", "q_ge"),
               w=(dstqkey + str(c),))

    quant_weight(w1T_in, NCH, HID, w1q, "w1q", 0)
    es.close()
    P.barrier()

    es = ExitStack()
    W = TB + 2
    b_xq = sb("b_xq", [128, NCH * W], BF16, es=es)
    b_h = sb("b_h", [128, NHC * W], es=es)
    b_hs = sb("b_hs", [128, NHC * TB], es=es)
    b_abx = [sb(f"b_abx{i}", [128, TB], es=es) for i in range(2)]
    b_fmax = sb("b_fmax", [128, TB], es=es)
    b_t0 = [sb(f"b_t0{i}", [128, TB], es=es) for i in range(4)]
    b_t1 = [sb(f"b_t1{i}", [128, TB], es=es) for i in range(4)]
    b_rb = sb("b_rb", [128, W], es=es)
    b_sb2 = sb("b_sb2", [128, TB], es=es)
    b_s2 = sb("b_s2", [128, 2], es=es)
    b_mx = sb("b_mx", [128, 1], es=es)

    for b in range(NB):
        for hb in range(S // TB):
            t0 = hb * TB
            lo = max(t0 - 1, 0)
            hi = min(t0 + TB + 1, S)
            width = hi - lo
            loff = 1 - (t0 - lo)
            for c in range(NCH):
                if loff > 0:
                    vmemset(b_xq[:, c * W: c * W + 1], 0.0, w=("b_xq",))
                if loff + width < W:
                    vmemset(b_xq[:, c * W + W - 1: c * W + W], 0.0, w=("b_xq",))
                ldma(b_xq[:, c * W + loff: c * W + loff + width],
                     xq_h[b, c * 128:(c + 1) * 128, lo:hi], r=("xqh",), w=("b_xq",))
            build_bcast(r1tok, "r1tok", b, lo, width, loff, b_rb, "b_rb", W, rwcol=0)
            allh = ("b_h0", "b_h1", "b_h2", "b_h3")
            for hc in range(NHC):
                bank = 5 if hc % 2 == 0 else 3
                for c in range(NCH):
                    mm(pp[bank][:, :W], w1q[:, c * HID + hc * 128: c * HID + hc * 128 + 128],
                       b_xq[:, c * W:(c + 1) * W], start=(c == 0), stop=(c == NCH - 1),
                       r=(f"w1q{c}", "b_xq"), w=(f"ps{bank}",))
                tt(b_h[:, hc * W:(hc + 1) * W], pp[bank][:, :W], b_rb[:, :W], ALU.mult,
                   r=(f"ps{bank}", "b_rb"), w=(f"b_h{hc // 8}",))
            if t0 == 0:
                vmemset(bass.AP(b_h, 0, [[NHC * W, 128], [W, NHC], [1, 1]]), 0.0,
                        r=(), w=allh)
            if hi == S:
                vmemset(bass.AP(b_h, W - 1, [[NHC * W, 128], [W, NHC], [1, 1]]), 0.0,
                        r=(), w=allh)
            def emit_taps(hc):
                par = hc % 4
                hk = f"b_h{hc // 8}"
                act(b_t0[par][:, :], b_h[:, hc * W: hc * W + TB], AF.Identity,
                    scale=s_cw0[:, hc:hc + 1], r=(hk, "s_cw0"), w=(f"b_t0{par}",))
                act(b_t1[par][:, :], b_h[:, hc * W + 1: hc * W + 1 + TB],
                    AF.Identity, scale=s_cw1[:, hc:hc + 1],
                    r=(hk, "s_cw1"), w=(f"b_t1{par}",))

            def emit_vadds(hc):
                par = hc % 4
                hk = f"b_h{hc // 8}"
                tt(b_t0[par][:, :], b_t0[par][:, :], b_t1[par][:, :], ALU.add,
                   r=(f"b_t0{par}", f"b_t1{par}"), w=(f"b_t0{par}",))
                stt(b_t0[par][:, :], b_h[:, hc * W + 2: hc * W + 2 + TB],
                    s_cw2[:, hc:hc + 1], b_t0[par][:, :], ALU.mult, ALU.add,
                    r=(hk, "s_cw2", f"b_t0{par}"), w=(f"b_t0{par}",))

            def emit_silu_abs(hc):
                par = hc % 4
                hsk = f"b_hs{hc // 8}"
                abk = f"b_abx{hc % 2}"
                act(b_hs[:, hc * TB:(hc + 1) * TB], b_t0[par][:, :], AF.Silu,
                    bias=s_cb[:, hc:hc + 1], r=(f"b_t0{par}", "s_cb"), w=(hsk,))
                act(b_abx[hc % 2][:, :], b_hs[:, hc * TB:(hc + 1) * TB], AF.Abs,
                    r=(hsk,), w=(abk,))
                sdma(h_h[b, hc * 128:(hc + 1) * 128, t0:t0 + TB],
                     b_hs[:, hc * TB:(hc + 1) * TB], r=(hsk,), w=("hh",))
                if hc == 0:
                    vcopy(b_fmax[:, :], b_abx[0][:, :], r=(abk,), w=("b_fmax",))
                else:
                    tt(b_fmax[:, :], b_fmax[:, :], b_abx[hc % 2][:, :], ALU.max,
                       r=("b_fmax", abk), w=("b_fmax",))

            for hc in range(NHC + 1):
                if hc < NHC:
                    emit_taps(hc)
                if hc >= 1:
                    emit_silu_abs(hc - 1)
                if hc < NHC:
                    emit_vadds(hc)
            for sub in range(TB // 128):
                col = b * 16 + hb * (TB // 128) + sub
                transp(pp[2][:, 0:128], b_fmax[:, sub * 128:(sub + 1) * 128],
                       r=("b_fmax",), w=("ps2",))
                vred(b_mx[:, 0:1], pp[2][:, 0:128], ALU.max, r=("ps2",), w=("b_mx",))
                ts1(b_mx[:, 0:1], b_mx[:, 0:1], 1e-5, ALU.max, r=("b_mx",), w=("b_mx",))
                vrecip(b_s2[:, 0:1], b_mx[:, 0:1], r=("b_mx",), w=("b_s2",))
                ts1(b_s2[:, 0:1], b_s2[:, 0:1], 127.0, ALU.mult, r=("b_s2",), w=("b_s2",))
                vrecip(r2tok[:, col:col + 1], b_s2[:, 0:1], r=("b_s2",), w=("r2tok",))
                ts1(s2tok[:, col:col + 1], b_s2[:, 0:1], -1.0, ALU.mult,
                    r=("b_s2",), w=("s2tok",))
    es.close()
    es_w1.close()
    P.barrier()

    es_w2 = ExitStack()
    w2q = sb("w2q", [128, NHC * D], BF16, es=es_w2)
    es = ExitStack()
    q_wf = sb("q_wf2", [128, D], es=es)
    q_wg = sb("q_wg2", [128, D], es=es)
    qw2 = [(q_wf, "q_wf2"), (q_wg, "q_wg2")]
    q_v = sb("q_v2", [128, D], es=es)
    q_ge = sb("q_ge2", [128, D], es=es)
    q_r = sb("q_r2", [128, 3], es=es)
    quant_weight(w2T_in, NHC, D, w2q, "w2q", 1)
    es.close()
    P.barrier()

    es = ExitStack()
    TB2 = 256
    d_h = sb("d_h", [128, NHC * TB2], es=es)
    d_hq = sb("d_hq", [128, NHC * TB2], BF16, es=es)
    d_u0 = sb("d_u0", [128, NCH * TB2], es=es)
    d_i32 = sb("d_i32", [128, NCH * TB2], I32, es=es)
    d_x1 = sb("d_x1", [128, NCH * TB2], es=es)
    d_rb = sb("d_rb", [128, TB2], es=es)
    d_sb = sb("d_sb", [128, TB2], es=es)
    d_t2 = [sb(f"d_t{i}", [128, TB2], es=es) for i in range(2)]
    for b in range(NB):
        for hb in range(S // TB2):
            t0 = hb * TB2
            for hc in range(NHC):
                ldma(d_h[:, hc * TB2:(hc + 1) * TB2],
                     h_h[b, hc * 128:(hc + 1) * 128, t0:t0 + TB2],
                     r=("hh",), w=("d_h",))
            for c in range(NCH):
                ldma(d_x1[:, c * TB2:(c + 1) * TB2],
                     x1_h[b, c * 128:(c + 1) * 128, t0:t0 + TB2],
                     r=("x1h",), w=("d_x1",))
            build_bcast(r2tok, "r2tok", b, t0, TB2, 0, d_rb, "d_rb", TB2, rwcol=1)
            build_bcast(s2tok, "s2tok", b, t0, TB2, 0, d_sb, "d_sb", TB2)
            for grp in range(4):
                hc0 = grp * 8
                tt(d_u0[:, :NCH * TB2], d_h[:, hc0 * TB2:(hc0 + 8) * TB2],
                   bass.AP(d_sb, 0, [[TB2, 128], [0, 8], [1, TB2]]),
                   ALU.mult, r=("d_h", "d_sb"), w=("d_u0",))
                act(d_i32[:, :NCH * TB2], d_u0[:, :NCH * TB2], AF.Identity,
                    r=("d_u0",), w=("d_i32",))
                vcopy(d_hq[:, hc0 * TB2:(hc0 + 8) * TB2], d_i32[:, :NCH * TB2],
                      r=("d_i32",), w=("d_hq",))
            for oc in range(NCH):
                bank = 5 if oc % 2 == 0 else 3
                for kc in range(NHC):
                    mm(pp[bank][:, :TB2], w2q[:, kc * D + oc * 128: kc * D + oc * 128 + 128],
                       d_hq[:, kc * TB2:(kc + 1) * TB2], start=(kc == 0),
                       stop=(kc == NHC - 1), r=(f"w2q{kc}", "d_hq"), w=(f"ps{bank}",))
                dt = d_t2[oc % 2]
                dk = f"d_t{oc % 2}"
                tt(dt[:, :], pp[bank][:, :TB2], d_rb[:, :TB2], ALU.mult,
                   r=(f"ps{bank}", "d_rb"), w=(dk,))
                tt(dt[:, :], dt[:, :], d_x1[:, oc * TB2:(oc + 1) * TB2], ALU.add,
                   r=(dk, "d_x1"), w=(dk,))
                sdma(s1_h[b, oc * 128:(oc + 1) * 128, t0:t0 + TB2], dt[:, :],
                     r=(dk,), w=("s1h",))
    es.close()
    es_w2.close()
    P.barrier()

    es = ExitStack()
    t_pT = sb("t_pT", [128, 16 * 512], es=es)
    t_vT = sb("t_vT", [128, 16 * HEAD], es=es)
    t_q = sb("t_q", [HEAD, S], es=es)
    t_k = sb("t_k", [HEAD, S], es=es)
    t_v = sb("t_v", [HEAD, S], es=es)
    t_rm = sb("t_rm", [128, 1], es=es)
    t_rm2 = sb("t_rm2", [128, 1], es=es)
    t_cn = sb("t_cn", [128, 16], es=es)
    t_z = sb("t_z", [HEAD, 512], es=es)
    t_rb = sb("t_rb", [128, 512], es=es)
    t_s1 = sb("t_s1", [128, NCH * 512], es=es)
    t_o2 = [sb(f"t_o{i}", [128, 512], es=es) for i in range(2)]
    t_val = sb("t_val", [1, 4], es=es)

    QB = 512
    for b in range(NB):
        qo = 0
        ldma(t_q[0:HEAD, :], qkv_h[0, b, 0:HEAD, :], r=("qkvh",), w=("t_q",))
        ldma(t_k[0:HEAD, :], qkv_h[1, b, 0:HEAD, :], r=("qkvh",), w=("t_k",))
        ldma(t_v[0:HEAD, :], qkv_h[2, b, 0:HEAD, :], r=("qkvh",), w=("t_v",))
        for kt in range(16):
            transp(pp[2][:, 0:HEAD], t_v[0:HEAD, qo + kt * 128: qo + (kt + 1) * 128],
                   r=("t_v",), w=("ps2",), npart=HEAD)
            vcopy(t_vT[:, kt * HEAD:(kt + 1) * HEAD], pp[2][:, 0:HEAD],
                  r=("ps2",), w=("t_vT",))
        for J in range(S // QB):
            q0 = J * QB
            nkt = 4 * J + 4
            for i in range(nkt):
                pkey = f"t_pT{i}"
                bank = 5 if i % 2 == 0 else 2
                mm(pp[bank][:, :QB], t_k[0:HEAD, qo + i * 128: qo + i * 128 + 128],
                   t_q[0:HEAD, qo + q0: qo + q0 + QB], start=True, stop=True,
                   r=("t_k", "t_q"), w=(f"ps{bank}",))
                ts1(pp[bank][:, :QB], pp[bank][:, :QB], 0.125, ALU.mult,
                    r=(f"ps{bank}",), w=(f"ps{bank}",))
                for jj in range(4):
                    j = 4 * J + jj
                    cs = jj * 128
                    if j < i:
                        vmemset(t_pT[:, i * 512 + cs: i * 512 + cs + 128], 0.0,
                                w=(pkey,))
                    else:
                        if j == i:
                            tt(pp[bank][:, cs:cs + 128], pp[bank][:, cs:cs + 128],
                               c_mT[:, :], ALU.add, r=(f"ps{bank}", "c_mT"),
                               w=(f"ps{bank}",))
                        act(t_pT[:, i * 512 + cs: i * 512 + cs + 128],
                            pp[bank][:, cs:cs + 128], AF.Exp,
                            bias=c_c[:, C_M20:C_M20 + 1],
                            r=(f"ps{bank}", "c_c"), w=(pkey,))
            for i in range(nkt):
                mm(pr[0][:1, :QB], c_o128[:, :1], t_pT[:, i * 512: i * 512 + QB],
                   start=(i == 0), stop=(i == nkt - 1),
                   r=(f"t_pT{i}", "c_o128"), w=("ps6",))
            vcopy(rowt[:1, :QB], pr[0][:1, :QB], r=("ps6",), w=("rowt",))
            vrecip(rowt[:1, :QB], rowt[:1, :QB], r=("rowt",), w=("rowt",))
            mm(pp[3][:, :QB], c_o1[:1, :], rowt[:1, :QB], start=True, stop=True,
               r=("rowt", "c_o1"), w=("ps3",))
            vcopy(t_rb[:, :QB], pp[3][:, :QB], r=("ps3",), w=("t_rb",))
            for i in range(nkt):
                mm(pp[4][0:HEAD, :QB], t_vT[:, i * HEAD:(i + 1) * HEAD],
                   t_pT[:, i * 512: i * 512 + QB], start=(i == 0),
                   stop=(i == nkt - 1), r=("t_vT", f"t_pT{i}"), w=("ps4",))
            tt(t_z[0:HEAD, :QB], pp[4][0:HEAD, :QB], t_rb[0:HEAD, :QB], ALU.mult,
               r=("ps4", "t_rb"), w=("t_z",))
            for c in range(NCH):
                ldma(t_s1[:, c * QB:(c + 1) * QB],
                     s1_h[b, c * 128:(c + 1) * 128, q0:q0 + QB],
                     r=("s1h",), w=("t_s1",))
            for oc in range(NCH):
                bank = 5 if oc % 2 == 0 else 2
                ok = f"t_o{oc % 2}"
                to = t_o2[oc % 2]
                mm(pp[bank][:, :QB], s_wo[0:HEAD, oc * 128:(oc + 1) * 128],
                   t_z[0:HEAD, :QB], start=True, stop=True,
                   r=("s_wo", "t_z"), w=(f"ps{bank}",))
                act(to[:, :QB], pp[bank][:, :QB], AF.Identity, bias=s_boc[:, oc:oc + 1],
                    r=(f"ps{bank}", "s_boc"), w=(ok,))
                tt(to[:, :QB], to[:, :QB], t_s1[:, oc * QB:(oc + 1) * QB], ALU.add,
                   r=(ok, "t_s1"), w=(ok,))
                sdma(out_fm[b, oc * 128:(oc + 1) * 128, q0:q0 + QB], to[:, :QB],
                     r=(ok,), w=("OUT",))
    es.close()

    for e in P.ENGS:
        for node in P.ins[e]:
            for (we, wi) in node["waits"]:
                P.ins[we][wi]["inc"] = True
        for node in P.ins[e]:
            if node["dma"]:
                node["inc"] = True
    fams = []
    for e in P.ENGS:
        for node in P.ins[e]:
            if node["dma"] and node["fam"] not in fams:
                fams.append(node["fam"])
    fence = {}
    cnts = {}
    for e in P.ENGS:
        for i, node in enumerate(P.ins[e]):
            if node["dma"]:
                sn = "f_" + str(node["fam"])
                cnts[sn] = cnts.get(sn, 0) + 16
                fence[(e, i)] = (sn, cnts[sn])
            else:
                sn = "e_" + e
                if node["inc"]:
                    cnts[sn] = cnts.get(sn, 0) + 1
                fence[(e, i)] = (sn, cnts.get(sn, 0))
    final_store_totals = [("f_" + f, cnts.get("f_" + f, 0)) for f in
                          ("OUT", "x1h", "s1h", "hh", "hh2", "xn0h", "moeh", "qkvh", "xqh")
                          if "f_" + f in cnts]

    es_sem = ExitStack()
    sems = {}
    for e in P.ENGS:
        sems["e_" + e] = es_sem.enter_context(nc.semaphore(f"sem_{e}"))
    for f in fams:
        sems["f_" + str(f)] = es_sem.enter_context(nc.semaphore(f"semf_{f}"))

    with es_sem, nc.Block() as block:
        def run_engine(engobj, ename):
            maxw = {}
            for node in P.ins[ename]:
                wlist = sorted(fence[k] for k in node["waits"])
                for (sn, val) in wlist:
                    if maxw.get(sn, -1) < val:
                        engobj.wait_ge(sems[sn], val)
                        maxw[sn] = val
                ins = node["fn"](engobj)
                if node["inc"]:
                    sn = ("f_" + str(node["fam"])) if node["dma"] else ("e_" + ename)
                    ins.then_inc(sems[sn], 16 if node["dma"] else 1)

        @block.sync
        def _(sync):
            run_engine(sync, "sy")

        @block.tensor
        def _(tensor):
            run_engine(tensor, "te")

        @block.vector
        def _(vector):
            run_engine(vector, "ve")

        @block.scalar
        def _(scalar):
            run_engine(scalar, "ac")

        @block.gpsimd
        def _(gpsimd):
            run_engine(gpsimd, "gp")
            for (sn, val) in final_store_totals:
                gpsimd.wait_ge(sems[sn], val)

    es_top.close()
    return nc


def kernel(L0_attn_x, L0_attn_W1, L0_attn_b1, L0_attn_W2, L0_attn_b2,
           L0_attn_W3, L0_attn_b3, L0_attn_A,
           L0_ffn_router_indices, L0_ffn_routing_weights,
           L0_ffn_gate_up_proj, L0_ffn_gate_up_proj_bias,
           L0_ffn_down_proj, L0_ffn_down_proj_bias,
           L1_attn_wq, L1_attn_bq, L1_attn_wk, L1_attn_bk,
           L1_attn_wv, L1_attn_bv, L1_attn_wo, L1_attn_bo,
           L1_ffn_w1, L1_ffn_conv_w, L1_ffn_conv_b, L1_ffn_w2):
    f32 = np.float32

    def cc(a):
        return np.ascontiguousarray(np.asarray(a), dtype=f32)

    x = np.asarray(L0_attn_x)
    consts = np.broadcast_to(
        np.array([1e-5, -1.0, -0.5, 1.702, -20.0, 0, 0, 0], f32), (128, 8)).copy()
    ident = np.eye(128, dtype=f32)
    onesc = np.ones((1, 128), f32)
    ones128 = np.ones((128, 1), f32)
    tri = np.triu(np.ones((128, 128), f32), 1) * f32(-100000000000.0)
    mask1 = cc(tri)
    maskT = cc(tri.T)
    ridx = np.asarray(L0_ffn_router_indices)
    rw_all = np.asarray(L0_ffn_routing_weights)
    W1T = cc(np.asarray(L0_attn_W1).T)
    b1c = cc(np.asarray(L0_attn_b1).reshape(NCH, 128).T)
    w2t = cc(np.asarray(L0_attn_W2).T)
    w3t = cc(np.asarray(L0_attn_W3).T)
    b23 = cc(np.stack([np.asarray(L0_attn_b2), np.asarray(L0_attn_b3)], axis=1))
    gup = np.asarray(L0_ffn_gate_up_proj)
    perm = np.concatenate([np.arange(0, 2048, 2), np.arange(1, 2048, 2)])
    gupT = cc(gup[:, :, perm])
    gub = np.asarray(L0_ffn_gate_up_proj_bias)[:, perm]
    gubc = cc(gub.reshape(NEXP, 16, 128).transpose(0, 2, 1))
    dwnT = cc(L0_ffn_down_proj)
    dbrow = cc(np.asarray(L0_ffn_down_proj_bias).reshape(NEXP, 1, D))
    xb0 = cc(x[0].T)
    idxs_all = np.ascontiguousarray(ridx[:S].T, dtype=np.int32)
    wq = cc(L1_attn_wq)
    wk = cc(L1_attn_wk)
    wv = cc(L1_attn_wv)
    bqkv = cc(np.stack([np.asarray(L1_attn_bq), np.asarray(L1_attn_bk),
                        np.asarray(L1_attn_bv)], axis=1))
    wo = cc(L1_attn_wo)
    boc = cc(np.asarray(L1_attn_bo).reshape(NCH, 128).T)
    w1T_in = cc(np.asarray(L1_ffn_w1).T)
    w2T_in = cc(np.asarray(L1_ffn_w2).T)
    cw = np.asarray(L1_ffn_conv_w)
    cwc = cc(np.stack([cw[:, 0, j].reshape(NHC, 128).T for j in range(3)]))
    cbc = cc(np.asarray(L1_ffn_conv_b).reshape(NHC, 128).T)


    nc = build_program()
    in_maps = []
    for core in range(NCORES):
        xc = cc(x[2 * core: 2 * core + 2].transpose(0, 2, 1))
        mask_moe = np.full((128, 1), 1.0 if core == 0 else 0.0, f32)
        eid = np.full((2, 1), -float(core), f32)
        rws = cc(rw_all[:S, core].reshape(1, S))
        in_maps.append(dict(
            x_fm=xc, xb0_fm=xb0, eid=eid, consts=consts, ident=ident,
            onesc=onesc, ones128=ones128, mask1=mask1, maskT=maskT,
            mask_moe=mask_moe, idxs=idxs_all, rws=rws, W1T=W1T, b1c=b1c,
            w2t=w2t, w3t=w3t, b23=b23,
            gupT=cc(gupT[core]), gubc=cc(gubc[core]), dwnT=cc(dwnT[core]),
            dbrow=cc(dbrow[core]), wq=wq, wk=wk, wv=wv,
            bqkv=bqkv, wo=wo, boc=boc, w1T_in=w1T_in, w2T_in=w2T_in, cwc=cwc,
            cbc=cbc))
    res = run_bass_kernel_spmd(nc, in_maps, core_ids=list(range(NCORES)))
    outs = []
    for core in range(NCORES):
        o = np.asarray(res.results[core]["out_fm"])
        outs.append(o.transpose(0, 2, 1))
    return np.ascontiguousarray(np.concatenate(outs, axis=0), dtype=np.float32)


# version 21
# speedup vs baseline: 1.0269x; 1.0269x over previous
import numpy as np
import concourse.bass as bass
import concourse.mybir as mybir
from concourse.bass_utils import run_bass_kernel_spmd

F32 = mybir.dt.float32
BF16 = mybir.dt.bfloat16
I32 = mybir.dt.int32
AF = mybir.ActivationFunctionType
ALU = mybir.AluOpType
AX = mybir.AxisListType

D = 1024
S = 2048
NB = 2
NCORES = 8
NCH = 8
HID = 4096
NHC = 32
TA = 512
TB = 256
MOE_TB = 512
NEXP = 8
HEAD = 64
LADDER = [128.0, 64.0, 32.0, 16.0, 8.0, 4.0, 2.0, 1.0]
C_EPS, C_NEG1, C_NEGHALF, C_1702, C_M20 = 0, 1, 2, 3, 4


class Prog:
    ENGS = ("te", "ve", "ac", "gp", "sy")

    def __init__(self):
        self.ins = {e: [] for e in self.ENGS}
        self.last_w = {}
        self.last_r = {}
        self.barrier_fences = []

    def emit(self, eng, fn, r=(), w=(), dma=False):
        idx = len(self.ins[eng])
        waits = set()
        for key in self.barrier_fences:
            if key[0] != eng:
                waits.add(key)
        for key in r:
            lw = self.last_w.get(key)
            if lw is not None:
                waits.add(lw)
        for key in w:
            for rr in self.last_r.get(key, ()):
                if rr[0] != eng:
                    waits.add(rr)
            lw = self.last_w.get(key)
            if lw is not None and lw[0] != eng:
                waits.add(lw)
        fam = w[0] if (dma and w) else None
        self.ins[eng].append({"fn": fn, "waits": waits, "inc": dma, "dma": dma,
                              "fam": fam})
        for key in w:
            self.last_w[key] = (eng, idx)
            self.last_r[key] = []
        for key in r:
            self.last_r.setdefault(key, []).append((eng, idx))

    def barrier(self):
        self.barrier_fences = [(e, len(self.ins[e]) - 1)
                               for e in self.ENGS if self.ins[e]]


def build_program():
    from contextlib import ExitStack
    nc = bass.Bass("TRN2", target_bir_lowering=False)
    P = Prog()

    def din(name, shape, dt=F32):
        return nc.dram_tensor(name, shape, dt, kind="ExternalInput").ap()

    x_fm = din("x_fm", [NB, D, S])
    xb0_fm = din("xb0_fm", [D, S])
    eid_i = din("eid", [2, 1])
    consts = din("consts", [128, 8])
    ident = din("ident", [128, 128])
    onesc = din("onesc", [1, 128])
    ones128 = din("ones128", [128, 1])
    mask1 = din("mask1", [128, 128])
    maskT = din("maskT", [128, 128])
    mask_moe = din("mask_moe", [128, 1])
    idxs = din("idxs", [2, S], I32)
    rws = din("rws", [1, S])
    W1T_i = din("W1T", [D, D])
    b1c_i = din("b1c", [128, NCH])
    w2t_i = din("w2t", [D, 16])
    w3t_i = din("w3t", [D, 16])
    b23_i = din("b23", [16, 2])
    gupT = din("gupT", [D, 2048])
    gubc = din("gubc", [128, 16])
    dwnT = din("dwnT", [D, D])
    dbrow = din("dbrow", [1, D])
    wq_i = din("wq", [D, HEAD])
    wk_i = din("wk", [D, HEAD])
    wv_i = din("wv", [D, HEAD])
    bqkv_i = din("bqkv", [HEAD, 3])
    wo_i = din("wo", [HEAD, D])
    boc_i = din("boc", [128, NCH])
    w1T_in = din("w1T_in", [D, HID])
    w2T_in = din("w2T_in", [HID, D])
    cwc_i = din("cwc", [3, 128, NHC])
    cbc_i = din("cbc", [128, NHC])

    out_fm = nc.dram_tensor("out_fm", [NB, D, S], F32, kind="ExternalOutput").ap()

    ag_in = nc.dram_tensor("ag_in", [D, S], F32).ap()
    ag_out = nc.dram_tensor("ag_out", [D, S], F32).ap()
    x1_h = nc.dram_tensor("x1_h", [NB, D, S], F32).ap()
    s1_h = nc.dram_tensor("s1_h", [NB, D, S], F32).ap()
    hq_h = nc.dram_tensor("hq_h", [NB, HID, S], BF16).ap()
    qkv_h = nc.dram_tensor("qkv_h", [3, NB, HEAD, S], F32).ap()
    xq_h = nc.dram_tensor("xq_h", [NB, D, S], BF16).ap()

    es_top = ExitStack()

    def sb(name, shape, dt=F32, es=None):
        return (es or es_top).enter_context(nc.sbuf_tensor(name, shape, dt))

    c_c = sb("c_c", [128, 8])
    c_id = sb("c_id", [128, 128])
    c_o1 = sb("c_o1", [1, 128])
    c_o128 = sb("c_o128", [128, 1])
    c_m1 = sb("c_m1", [128, 128])
    c_mT = sb("c_mT", [128, 128])
    c_mmoe = sb("c_mmoe", [128, 1])
    m2tok = sb("m2tok", [128, NB * 16])
    i2tok = sb("i2tok", [128, NB * 16])
    s1tok = sb("s1tok", [128, NB * 16])
    r1tok = sb("r1tok", [128, NB * 16])
    r2tok = sb("r2tok", [128, NB * 16])
    rowm = sb("rowm", [1, TA])
    rowi = sb("rowi", [1, TA])
    rowt = sb("rowt", [1, TA])
    rowt2 = sb("rowt2", [1, TA])
    val1 = sb("val1", [1, 4])
    wsc1 = sb("wsc1", [128, 2])
    rwsc = sb("rwsc", [1, 2])
    qr1 = sb("qr1", [128, 2])
    s_w2t = sb("s_w2t", [128, NCH * 16])
    s_w3t = sb("s_w3t", [128, NCH * 16])
    s_b23 = sb("s_b23", [16, 2])
    s_b1c = sb("s_b1c", [128, NCH])
    s_wq = sb("s_wq", [128, NCH * HEAD])
    s_wk = sb("s_wk", [128, NCH * HEAD])
    s_wv = sb("s_wv", [128, NCH * HEAD])
    s_bqkv = sb("s_bqkv", [HEAD, 3])
    s_wo = sb("s_wo", [HEAD, D])
    s_boc = sb("s_boc", [128, NCH])
    s_cw0 = sb("s_cw0", [128, NHC])
    s_cw1 = sb("s_cw1", [128, NHC])
    s_cw2 = sb("s_cw2", [128, NHC])
    s_cb = sb("s_cb", [128, NHC])

    pp = [es_top.enter_context(nc.psum_tensor(f"pp{i}", [128, 512], F32))
          for i in range(8)]
    pr = [pp[6], pp[7]]

    def ldma(dst, src, r=(), w=(), q="sy"):
        P.emit(q, lambda e, d=dst, s=src: e.dma_start(out=d, in_=s), r=r, w=w, dma=True)

    def sdma(dst, src, r=(), w=(), q="gp"):
        P.emit(q, lambda e, d=dst, s=src: e.dma_start(out=d, in_=s), r=r, w=w, dma=True)

    def mm(out, lhsT, rhs, start, stop, r=(), w=()):
        P.emit("te", lambda e, o=out, l=lhsT, x=rhs, st=start, sp=stop:
               e.matmul(o, l, x, start=st, stop=sp), r=r, w=w)

    def transp(out, in_, r=(), w=(), npart=128):
        P.emit("te", lambda e, o=out, i=in_, n=npart:
               e.transpose(o, i, c_id[0:n, 0:n]),
               r=tuple(r) + ("c_id",), w=w)

    def act(out, in_, func, bias=0.0, scale=1.0, r=(), w=()):
        P.emit("ac", lambda e, o=out, i=in_, f=func, b=bias, s=scale:
               e.activation(o, i, f, bias=b, scale=s), r=r, w=w)

    def tt(out, a, b, op, r=(), w=()):
        P.emit("ve", lambda e, o=out, x=a, y=b, p=op: e.tensor_tensor(o, x, y, p),
               r=r, w=w)

    def ts(out, in_, s1, s2, op0, op1, r=(), w=()):
        P.emit("ve", lambda e, o=out, i=in_, a=s1, b=s2, p0=op0, p1=op1:
               e.tensor_scalar(o, i, a, b, p0, p1), r=r, w=w)

    def ts1(out, in_, s1, op, r=(), w=()):
        P.emit("ve", lambda e, o=out, i=in_, a=s1, p=op:
               e.tensor_single_scalar(o, i, a, p), r=r, w=w)

    def tsm(out, in_, sap, r=(), w=()):
        P.emit("ve", lambda e, o=out, i=in_, a=sap: e.tensor_scalar_mul(o, i, a),
               r=r, w=w)

    def stt(out, in0, scalar, in1, op0, op1, r=(), w=()):
        P.emit("ve", lambda e, o=out, i0=in0, sc=scalar, i1=in1, p0=op0, p1=op1:
               e.scalar_tensor_tensor(o, i0, sc, i1, p0, p1), r=r, w=w)

    def vcopy(out, in_, r=(), w=()):
        P.emit("ve", lambda e, o=out, i=in_: e.tensor_copy(o, i), r=r, w=w)

    def vred(out, in_, op, absval=False, r=(), w=()):
        P.emit("ve", lambda e, o=out, i=in_, p=op, av=absval:
               e.tensor_reduce(o, i, AX.X, p, apply_absolute_value=av), r=r, w=w)

    def vrecip(out, in_, r=(), w=()):
        P.emit("ve", lambda e, o=out, i=in_: e.reciprocal(o, i), r=r, w=w)

    def vmemset(ap_, val, r=(), w=()):
        P.emit("ve", lambda e, o=ap_, v=val: e.memset(o, v), r=r, w=w)

    ldma(c_c[:, :], consts[:, :], w=("c_c",))
    ldma(c_id[:, :], ident[:, :], w=("c_id",))
    ldma(c_o1[:, :], onesc[:, :], w=("c_o1",))
    ldma(c_o128[:, :], ones128[:, :], w=("c_o128",))
    ldma(c_m1[:, :], mask1[:, :], w=("c_m1",))
    ldma(c_mT[:, :], maskT[:, :], w=("c_mT",))
    ldma(c_mmoe[:, :], mask_moe[:, :], w=("c_mmoe",))
    for c in range(NCH):
        ldma(s_w2t[:, c * 16:(c + 1) * 16], w2t_i[c * 128:(c + 1) * 128, :], w=("s_w2t",))
        ldma(s_w3t[:, c * 16:(c + 1) * 16], w3t_i[c * 128:(c + 1) * 128, :], w=("s_w3t",))
        ldma(s_wq[:, c * HEAD:(c + 1) * HEAD], wq_i[c * 128:(c + 1) * 128, :], w=("s_wq",))
        ldma(s_wk[:, c * HEAD:(c + 1) * HEAD], wk_i[c * 128:(c + 1) * 128, :], w=("s_wk",))
        ldma(s_wv[:, c * HEAD:(c + 1) * HEAD], wv_i[c * 128:(c + 1) * 128, :], w=("s_wv",))
    ldma(s_b23[:, :], b23_i[:, :], w=("s_b23",))
    ldma(s_b1c[:, :], b1c_i[:, :], w=("s_b1c",))
    ldma(s_bqkv[:, :], bqkv_i[:, :], w=("s_bqkv",))
    ldma(s_wo[:, :], wo_i[:, :], w=("s_wo",))
    ldma(s_boc[:, :], boc_i[:, :], w=("s_boc",))
    ldma(s_cw0[:, :], cwc_i[0, :, :], w=("s_cw0",))
    ldma(s_cw1[:, :], cwc_i[1, :, :], w=("s_cw1",))
    ldma(s_cw2[:, :], cwc_i[2, :, :], w=("s_cw2",))
    ldma(s_cb[:, :], cbc_i[:, :], w=("s_cb",))

    def emit_ln(x_t, xkey, T, xn_t, xnkey, sq_t, sqkey):
        for c in range(NCH):
            mm(pr[0][:1, :T], c_o128[:, :1], x_t[:, c * T:(c + 1) * T],
               start=(c == 0), stop=(c == NCH - 1), r=(xkey, "c_o128"), w=("ps6",))
        for c in range(NCH):
            k = c % 2
            act(sq_t[:, k * T:(k + 1) * T], x_t[:, c * T:(c + 1) * T], AF.Square,
                r=(xkey,), w=(sqkey + str(k),))
            mm(pr[1][:1, :T], c_o128[:, :1], sq_t[:, k * T:(k + 1) * T],
               start=(c == 0), stop=(c == NCH - 1),
               r=(sqkey + str(k), "c_o128"), w=("ps7",))
        ts1(rowm[:1, :T], pr[0][:1, :T], 1.0 / D, ALU.mult, r=("ps6",), w=("rowm",))
        tt(rowt2[:1, :T], rowm[:1, :T], rowm[:1, :T], ALU.mult, r=("rowm",), w=("rowt2",))
        stt(rowt2[:1, :T], pr[1][:1, :T], 1.0 / D, rowt2[:1, :T],
            ALU.mult, ALU.subtract, r=("ps7", "rowt2"), w=("rowt2",))
        act(rowt2[:1, :T], rowt2[:1, :T], AF.Ln, bias=c_c[:1, C_EPS:C_EPS + 1],
            r=("rowt2", "c_c"), w=("rowt2",))
        act(rowi[:1, :T], rowt2[:1, :T], AF.Exp,
            scale=c_c[:1, C_NEGHALF:C_NEGHALF + 1], r=("rowt2", "c_c"), w=("rowi",))
        mm(pp[0][:, :T], c_o1[:1, :], rowm[:1, :T], start=True, stop=True,
           r=("rowm", "c_o1"), w=("ps0",))
        mm(pp[1][:, :T], c_o1[:1, :], rowi[:1, :T], start=True, stop=True,
           r=("rowi", "c_o1"), w=("ps1",))
        for c in range(NCH):
            tt(xn_t[:, c * T:(c + 1) * T], x_t[:, c * T:(c + 1) * T], pp[0][:, :T],
               ALU.subtract, r=(xkey, "ps0"), w=(xnkey,))
            tt(xn_t[:, c * T:(c + 1) * T], xn_t[:, c * T:(c + 1) * T], pp[1][:, :T],
               ALU.mult, r=(xnkey, "ps1"), w=(xnkey,))

    def row_to_tok(row_t, rkey, T, dst, dkey, col0):
        for sub in range(T // 128):
            transp(pp[2][:, 0:1], row_t[:1, sub * 128:(sub + 1) * 128],
                   r=(rkey,), w=("ps2",), npart=1)
            vcopy(dst[:, col0 + sub:col0 + sub + 1], pp[2][:, 0:1],
                  r=("ps2",), w=(dkey,))

    def build_bcast(toksrc, tkey, b, lo, width, loff, dst, dkey, dstw, rwcol=None):
        if loff > 0 or loff + width < dstw:
            vmemset(dst[:, :dstw], 1.0, w=(dkey,))
        tl0 = lo // 128
        ntl = (lo + width - 1) // 128 - tl0 + 1
        for j in range(ntl):
            transp(pp[2][0:1, 0:128],
                   toksrc[:, b * 16 + tl0 + j: b * 16 + tl0 + j + 1],
                   r=(tkey,), w=("ps2",))
            vcopy(rowt[:1, j * 128:(j + 1) * 128], pp[2][0:1, 0:128],
                  r=("ps2",), w=("rowt",))
        off = lo - tl0 * 128
        if rwcol is not None:
            P.emit("ve", lambda e, o=rowt[:1, off:off + width],
                   i=rowt[:1, off:off + width], a=rwsc[:1, rwcol:rwcol + 1]:
                   e.tensor_scalar_mul(o, i, a), r=("rowt", "rwsc"), w=("rowt",))
        mm(pp[3][:, :width], c_o1[:1, :], rowt[:1, off:off + width],
           start=True, stop=True, r=("rowt", "c_o1"), w=("ps3",))
        vcopy(dst[:, loff:loff + width], pp[3][:, :width], r=("ps3",), w=(dkey,))

    def ladder(u0_ap, u_ap, kt_ap, u0key, ukey, ktkey):
        ts1(u_ap, u0_ap, 0.0, ALU.add, r=(u0key,), w=(ukey,))
        for kv in LADDER:
            ts(kt_ap, u_ap, kv, kv, ALU.is_ge, ALU.mult, r=(ukey,), w=(ktkey,))
            tt(u_ap, u_ap, kt_ap, ALU.subtract, r=(ukey, ktkey), w=(ukey,))

    es = ExitStack()
    MB = 512
    m_gup = sb("m_gup", [128, NCH * 2048], es=es)
    m_dwn = sb("m_dwn", [128, NCH * 1024], es=es)
    m_x0 = sb("m_x0", [128, NCH * MB], es=es)
    m_xn0 = sb("m_xn0", [128, NCH * MB], es=es)
    m_int = sb("m_int", [128, NCH * MB], es=es)
    m_sq = sb("m_sq", [128, 2 * MB], es=es)
    m_g = sb("m_g", [128, MB], es=es)
    m_u = sb("m_u", [128, MB], es=es)
    m_do = [sb(f"m_do{i}", [128, MB], es=es) for i in range(2)]
    m_rw = sb("m_rw", [1, MB], es=es)
    m_coef = sb("m_coef", [1, MB], es=es)
    m_c2 = sb("m_c2", [1, MB], es=es)
    m_gub = sb("m_gub", [128, 16], es=es)
    m_dbr = sb("m_dbr", [1, D], es=es)
    m_eid = sb("m_eid", [2, 1], es=es)
    m_idx = sb("m_idx", [1, 2 * MB], I32, es=es)
    g_idxf = sb("g_idxf", [1, 2 * MB], es=es)
    g_d = sb("g_d", [1, 2 * MB], es=es)
    g_t = sb("g_t", [1, 2 * MB], es=es)

    for c in range(NCH):
        ldma(m_gup[:, c * 2048:(c + 1) * 2048],
             gupT[c * 128:(c + 1) * 128, :], w=("m_gup",))
        ldma(m_dwn[:, c * 1024:(c + 1) * 1024],
             dwnT[c * 128:(c + 1) * 128, :], w=("m_dwn",))
    ldma(m_gub[:, :], gubc[:, :], w=("m_gub",))
    ldma(m_dbr[:1, :], dbrow[:, :], w=("m_dbr",))
    ldma(m_eid[0:2, :], eid_i[:, :], w=("m_eid",))
    for ti in range(S // MB):
        t0 = ti * MB
        for c in range(NCH):
            ldma(m_x0[:, c * MB:(c + 1) * MB],
                 xb0_fm[c * 128:(c + 1) * 128, t0:t0 + MB], w=("m_x0",))
        ldma(m_idx[0:1, 0:MB], idxs[0:1, t0:t0 + MB], w=("m_idx",))
        ldma(m_idx[0:1, MB:2 * MB], idxs[1:2, t0:t0 + MB], w=("m_idx",))
        ldma(m_rw[:1, :], rws[:1, t0:t0 + MB], w=("m_rw",))
        vcopy(g_idxf[0:1, :], m_idx[0:1, :], r=("m_idx",), w=("g_idxf",))
        emit_ln(m_x0, "m_x0", MB, m_xn0, "m_xn0", m_sq, "m_sq")
        act(g_d[0:1, :], g_idxf[0:1, :], AF.Identity, bias=m_eid[0:1, 0:1],
            r=("g_idxf", "m_eid"), w=("g_d",))
        ts1(g_t[0:1, :], g_d[0:1, :], -0.5, ALU.is_ge, r=("g_d",), w=("g_t",))
        stt(g_t[0:1, :], g_d[0:1, :], 0.5, g_t[0:1, :], ALU.is_le, ALU.mult,
            r=("g_d", "g_t"), w=("g_t",))
        tt(m_coef[:1, :], g_t[0:1, 0:MB], g_t[0:1, MB:2 * MB], ALU.add,
           r=("g_t",), w=("m_coef",))
        tt(m_coef[:1, :], m_coef[:1, :], m_rw[:1, :], ALU.mult,
           r=("m_coef", "m_rw"), w=("m_coef",))
        ts1(m_coef[:1, :], m_coef[:1, :], 1.0 / 1.702, ALU.mult,
            r=("m_coef",), w=("m_coef",))
        mm(pp[4][:, :MB], c_o1[:1, :], m_coef[:1, :],
           start=True, stop=True, r=("m_coef", "c_o1"), w=("ps4",))
        gb = [2, 5, 6]
        for g in range(NCH):
            for (pi, oc, dst, dkey) in ((0, g, m_g, "m_g"),
                                        (1, g + NCH, m_u, "m_u")):
                bank = gb[(2 * g + pi) % 3]
                for c in range(NCH):
                    mm(pp[bank][:, :MB],
                       m_gup[:, c * 2048 + oc * 128: c * 2048 + oc * 128 + 128],
                       m_xn0[:, c * MB:(c + 1) * MB],
                       start=(c == 0), stop=(c == NCH - 1),
                       r=("m_gup", "m_xn0"), w=(f"ps{bank}",))
                act(dst[:, :MB], pp[bank][:, :MB], AF.Identity,
                    bias=m_gub[:, oc:oc + 1],
                    r=(f"ps{bank}", "m_gub"), w=(dkey,))
            ts1(m_g[:, :MB], m_g[:, :MB], 7.0, ALU.min,
                r=("m_g",), w=("m_g",))
            act(m_g[:, :MB], m_g[:, :MB], AF.Silu,
                scale=c_c[:, C_1702:C_1702 + 1], r=("m_g", "c_c"), w=("m_g",))
            ts(m_u[:, :MB], m_u[:, :MB], -7.0, 7.0, ALU.max, ALU.min,
               r=("m_u",), w=("m_u",))
            stt(m_int[:, g * MB:(g + 1) * MB], m_u[:, :MB], 1.0,
                m_g[:, :MB], ALU.add, ALU.mult, r=("m_u", "m_g"), w=("m_int",))
            tt(m_int[:, g * MB:(g + 1) * MB],
               m_int[:, g * MB:(g + 1) * MB], pp[4][:, :MB],
               ALU.mult, r=("m_int", "ps4"), w=("m_int",))
        for oc in range(NCH):
            bank = 3 if oc % 2 == 0 else 7
            for c in range(NCH):
                mm(pp[bank][:, :MB],
                   m_dwn[:, c * 1024 + oc * 128: c * 1024 + oc * 128 + 128],
                   m_int[:, c * MB:(c + 1) * MB],
                   start=(c == 0), stop=False,
                   r=("m_dwn", "m_int"), w=(f"ps{bank}",))
            mm(pp[bank][:, :MB], m_dbr[:1, oc * 128:(oc + 1) * 128],
               m_coef[:1, :], start=False, stop=True,
               r=("m_dbr", "m_coef"), w=(f"ps{bank}",))
            dk = f"m_do{oc % 2}"
            vcopy(m_do[oc % 2][:, :], pp[bank][:, :MB], r=(f"ps{bank}",), w=(dk,))
            sdma(ag_in[oc * 128:(oc + 1) * 128, t0:t0 + MB], m_do[oc % 2][:, :],
                 r=(dk,), w=("agin",))
    P.emit("gp", lambda e: e.collective_compute(
        "AllReduce", ALU.add, replica_groups=[list(range(NCORES))],
        ins=[ag_in[:, :].opt()], outs=[ag_out[:, :].opt()]),
        r=("agin",), w=("agout",))
    es.close()
    P.barrier()

    es = ExitStack()
    a_x = sb("a_x2", [128, NCH * TA], es=es)
    a_xn = sb("a_xn2", [128, NCH * TA], es=es)
    a_sq = sb("a_sq2", [128, 2 * TA], es=es)
    a_W1 = sb("a_W1", [128, NCH * D], es=es)
    a_bc = sb("a_bc", [16, TA], es=es)
    a_cc = sb("a_cc", [16, TA], es=es)
    a_dl = sb("a_dl", [128, TA], es=es)
    a_t1 = sb("a_t1", [128, TA], es=es)
    a_moe = sb("a_moe", [128, TA], es=es)
    a_fmax = sb("a_fmax", [128, TA], es=es)
    a_fmin = sb("a_fmin", [128, TA], es=es)
    a_mx = sb("a_mx", [128, 2], es=es)
    a_qkv = sb("a_qkv", [HEAD, TA], es=es)
    a_u0 = sb("a_u0", [128, NCH * TA], es=es)
    a_i32 = sb("a_i32", [128, NCH * TA], I32, es=es)
    a_xqt = sb("a_xqt", [128, NCH * TA], BF16, es=es)

    for c in range(NCH):
        ldma(a_W1[:, c * D:(c + 1) * D], W1T_i[c * 128:(c + 1) * 128, :], w=("a_W1",))
    p_qw = [sb(f"p_qw{i}", [128, HID], es=es) for i in range(2)]
    p_qr = [sb(f"p_qr{i}", [128, 1], es=es) for i in range(2)]

    def emit_pass1(src_dram, cidx, chunk_w, wcol, first):
        qb = p_qw[cidx % 2]
        qk = f"p_qw{cidx % 2}"
        ldma(qb[:, :chunk_w], src_dram[cidx * 128:(cidx + 1) * 128, :],
             w=(qk,), q="sy" if cidx % 2 == 0 else "gp")
        qr = p_qr[cidx % 2]
        qrk = f"p_qr{cidx % 2}"
        vred(qr[:, 0:1], qb[:, :chunk_w], ALU.add, absval=True,
             r=(qk,), w=(qrk,))
        if first:
            vcopy(qr1[:, wcol:wcol + 1], qr[:, 0:1], r=(qrk,), w=("qr1",))
        else:
            tt(qr1[:, wcol:wcol + 1], qr1[:, wcol:wcol + 1], qr[:, 0:1],
               ALU.add, r=("qr1", qrk), w=("qr1",))

    for b in range(NB):
        for ti in range(S // TA):
            t0 = ti * TA
            for c in range(NCH):
                ldma(a_x[:, c * TA:(c + 1) * TA],
                     x_fm[b, c * 128:(c + 1) * 128, t0:t0 + TA], w=("a_x",))
            emit_ln(a_x, "a_x", TA, a_xn, "a_xn", a_sq, "a_sq")
            for c in range(NCH):
                mm(pp[2][0:16, :TA], s_w2t[:, c * 16:(c + 1) * 16],
                   a_xn[:, c * TA:(c + 1) * TA], start=(c == 0), stop=(c == NCH - 1),
                   r=("s_w2t", "a_xn"), w=("ps2",))
                mm(pp[3][0:16, :TA], s_w3t[:, c * 16:(c + 1) * 16],
                   a_xn[:, c * TA:(c + 1) * TA], start=(c == 0), stop=(c == NCH - 1),
                   r=("s_w3t", "a_xn"), w=("ps3",))
            act(a_bc[0:16, :], pp[2][0:16, :TA], AF.Identity, bias=s_b23[:, 0:1],
                r=("ps2", "s_b23"), w=("a_bc",))
            act(a_cc[0:16, :], pp[3][0:16, :TA], AF.Identity, bias=s_b23[:, 1:2],
                r=("ps3", "s_b23"), w=("a_cc",))
            tt(a_bc[0:16, :], a_bc[0:16, :], a_cc[0:16, :], ALU.mult,
               r=("a_bc", "a_cc"), w=("a_bc",))
            mm(pr[0][:1, :TA], c_o128[0:16, :1], a_bc[0:16, :], start=True, stop=True,
               r=("a_bc", "c_o128"), w=("ps6",))
            vcopy(rowt[:1, :TA], pr[0][:1, :TA], r=("ps6",), w=("rowt",))
            mm(pp[4][:, :TA], c_o1[:1, :], rowt[:1, :TA], start=True, stop=True,
               r=("rowt", "c_o1"), w=("ps4",))
            for oc in range(NCH):
                bank = 5 if oc % 2 == 0 else 3
                for c in range(NCH):
                    mm(pp[bank][:, :TA], a_W1[:, c * D + oc * 128: c * D + oc * 128 + 128],
                       a_xn[:, c * TA:(c + 1) * TA], start=(c == 0),
                       stop=(c == NCH - 1), r=("a_W1", "a_xn"), w=(f"ps{bank}",))
                act(a_t1[:, :], pp[bank][:, :TA], AF.Abs, bias=s_b1c[:, oc:oc + 1],
                    r=(f"ps{bank}", "s_b1c"), w=("a_t1",))
                act(a_t1[:, :], a_t1[:, :], AF.Exp, scale=c_c[:, C_NEG1:C_NEG1 + 1],
                    r=("a_t1", "c_c"), w=("a_t1",))
                act(a_t1[:, :], a_t1[:, :], AF.Ln, bias=1.0, r=("a_t1",), w=("a_t1",))
                act(a_dl[:, :], pp[bank][:, :TA], AF.Relu, bias=s_b1c[:, oc:oc + 1],
                    r=(f"ps{bank}", "s_b1c"), w=("a_dl",))
                tt(a_dl[:, :], a_dl[:, :], a_t1[:, :], ALU.add,
                   r=("a_dl", "a_t1"), w=("a_dl",))
                tt(a_dl[:, :], a_dl[:, :], a_xn[:, oc * TA:(oc + 1) * TA], ALU.mult,
                   r=("a_dl", "a_xn"), w=("a_dl",))
                tt(a_dl[:, :], a_dl[:, :], pp[4][:, :TA], ALU.mult,
                   r=("a_dl", "ps4"), w=("a_dl",))
                tt(a_x[:, oc * TA:(oc + 1) * TA], a_x[:, oc * TA:(oc + 1) * TA],
                   a_dl[:, :], ALU.add, r=("a_x", "a_dl"), w=("a_x",))
                if b == 0:
                    ldma(a_moe[:, :TA],
                         ag_out[oc * 128:(oc + 1) * 128, t0:t0 + TA],
                         r=("agout",), w=("a_moe",))
                    stt(a_x[:, oc * TA:(oc + 1) * TA], a_moe[:, :], c_mmoe[:, :1],
                        a_x[:, oc * TA:(oc + 1) * TA], ALU.mult, ALU.add,
                        r=("a_moe", "c_mmoe", "a_x"), w=("a_x",))
                sdma(x1_h[b, oc * 128:(oc + 1) * 128, t0:t0 + TA],
                     a_x[:, oc * TA:(oc + 1) * TA], r=("a_x",), w=("x1h",))
            emit_ln(a_x, "a_x", TA, a_xn, "a_xn", a_sq, "a_sq")
            row_to_tok(rowm, "rowm", TA, m2tok, "m2tok", b * 16 + ti * (TA // 128))
            row_to_tok(rowi, "rowi", TA, i2tok, "i2tok", b * 16 + ti * (TA // 128))
            vcopy(a_fmax[:, :], a_xn[:, 0:TA], r=("a_xn",), w=("a_fmax",))
            vcopy(a_fmin[:, :], a_xn[:, 0:TA], r=("a_xn",), w=("a_fmin",))
            for c in range(1, NCH):
                tt(a_fmax[:, :], a_fmax[:, :], a_xn[:, c * TA:(c + 1) * TA], ALU.max,
                   r=("a_fmax", "a_xn"), w=("a_fmax",))
                tt(a_fmin[:, :], a_fmin[:, :], a_xn[:, c * TA:(c + 1) * TA], ALU.min,
                   r=("a_fmin", "a_xn"), w=("a_fmin",))
            ts1(a_fmin[:, :], a_fmin[:, :], -1.0, ALU.mult, r=("a_fmin",), w=("a_fmin",))
            tt(a_fmax[:, :], a_fmax[:, :], a_fmin[:, :], ALU.max,
               r=("a_fmax", "a_fmin"), w=("a_fmax",))
            for sub in range(TA // 128):
                col = b * 16 + ti * (TA // 128) + sub
                transp(pp[2][:, 0:128], a_fmax[:, sub * 128:(sub + 1) * 128],
                       r=("a_fmax",), w=("ps2",))
                vred(a_mx[:, 0:1], pp[2][:, 0:128], ALU.max, r=("ps2",), w=("a_mx",))
                ts1(a_mx[:, 0:1], a_mx[:, 0:1], 1e-5, ALU.max, r=("a_mx",), w=("a_mx",))
                vrecip(a_mx[:, 1:2], a_mx[:, 0:1], r=("a_mx",), w=("a_mx",))
                ts1(s1tok[:, col:col + 1], a_mx[:, 1:2], 127.0, ALU.mult,
                    r=("a_mx",), w=("s1tok",))
                vrecip(r1tok[:, col:col + 1], s1tok[:, col:col + 1],
                       r=("s1tok",), w=("r1tok",))
                transp(pp[2][0:1, 0:128], s1tok[:, col:col + 1],
                       r=("s1tok",), w=("ps2",))
                vcopy(rowt2[:1, sub * 128:(sub + 1) * 128], pp[2][0:1, 0:128],
                      r=("ps2",), w=("rowt2",))
            mm(pp[3][:, :TA], c_o1[:1, :], rowt2[:1, :TA], start=True, stop=True,
               r=("rowt2", "c_o1"), w=("ps3",))
            ts1(a_dl[:, :TA], pp[3][:, :TA], -1.0, ALU.mult, r=("ps3",), w=("a_dl",))
            tt(a_u0[:, :], a_xn[:, :],
               bass.AP(a_dl, 0, [[TA, 128], [0, NCH], [1, TA]]), ALU.mult,
               r=("a_xn", "a_dl"), w=("a_u0",))
            for hp in range(2):
                hw = NCH * TA // 2
                act(a_i32[:, hp * hw:(hp + 1) * hw], a_u0[:, hp * hw:(hp + 1) * hw],
                    AF.Identity, r=("a_u0",), w=("a_i32",))
                vcopy(a_xqt[:, hp * hw:(hp + 1) * hw], a_i32[:, hp * hw:(hp + 1) * hw],
                      r=("a_i32",), w=("a_xqt",))
            for c in range(NCH):
                sdma(xq_h[b, c * 128:(c + 1) * 128, t0:t0 + TA],
                     a_xqt[:, c * TA:(c + 1) * TA], r=("a_xqt",), w=("xqh",))
            for (wt, wkey, bi) in ((s_wq, "s_wq", 0), (s_wk, "s_wk", 1),
                                   (s_wv, "s_wv", 2)):
                bank = 2 if bi % 2 == 0 else 5
                for c in range(NCH):
                    mm(pp[bank][0:HEAD, :TA], wt[:, c * HEAD:(c + 1) * HEAD],
                       a_xn[:, c * TA:(c + 1) * TA], start=(c == 0),
                       stop=(c == NCH - 1), r=(wkey, "a_xn"), w=(f"ps{bank}",))
                act(a_qkv[0:HEAD, :TA], pp[bank][0:HEAD, :TA],
                    AF.Identity, bias=s_bqkv[:, bi:bi + 1],
                    r=(f"ps{bank}", "s_bqkv"), w=("a_qkv",))
                sdma(qkv_h[bi, b, 0:HEAD, t0:t0 + TA], a_qkv[0:HEAD, :TA],
                     r=("a_qkv",), w=("qkvh",))
            tk = b * (S // TA) + ti
            emit_pass1(w1T_in, tk, HID, 0, first=(tk == 0))
            for c2 in range(4 * tk, 4 * tk + 4):
                emit_pass1(w2T_in, c2, D, 1, first=(c2 == 0))
    es.close()
    P.barrier()

    es_w1 = ExitStack()
    w1q = sb("w1q", [128, NCH * HID], BF16, es=es_w1)
    es = ExitStack()
    q_wf = sb("q_wf", [128, HID], es=es)
    q_wg = sb("q_wg", [128, HID], es=es)
    qw2 = [(q_wf, "q_wf"), (q_wg, "q_wg")]
    q_v = sb("q_v", [128, HID], es=es)
    q_ge = sb("q_ge", [128, HID], es=es)
    q_r = sb("q_r", [128, 3], es=es)

    def quant_weight(src_dram, nchunks, chunk_w, dstq, dstqkey, wcol):
        mm(pr[0][:1, 0:1], c_o128[:, :1], qr1[:, wcol:wcol + 1],
           start=True, stop=True, r=("qr1", "c_o128"), w=("ps6",))
        ts(val1[:1, 0:1], pr[0][:1, 0:1], 1.0 / (nchunks * 128 * chunk_w), 1e-5,
           ALU.mult, ALU.max, r=("ps6",), w=("val1",))
        vcopy(rwsc[:1, wcol:wcol + 1], val1[:1, 0:1], r=("val1",), w=("rwsc",))
        vrecip(val1[:1, 1:2], val1[:1, 0:1], r=("val1",), w=("val1",))
        mm(pp[2][:, 0:1], c_o1[:1, :], val1[:1, 1:2], start=True, stop=True,
           r=("val1", "c_o1"), w=("ps2",))
        vcopy(wsc1[:, wcol:wcol + 1], pp[2][:, 0:1], r=("ps2",), w=("wsc1",))
        for c in range(nchunks):
            qb, qk = qw2[c % 2]
            ldma(qb[:, :chunk_w], src_dram[c * 128:(c + 1) * 128, :],
                 w=(qk,), q="sy" if c % 2 == 0 else "gp")
            tsm(q_v[:, :chunk_w], qb[:, :chunk_w], wsc1[:, wcol:wcol + 1],
                r=(qk, "wsc1"), w=("q_v",))
            ts1(q_ge[:, :chunk_w], q_v[:, :chunk_w], 0.5, ALU.is_ge,
                r=("q_v",), w=("q_ge",))
            stt(q_ge[:, :chunk_w], q_v[:, :chunk_w], -0.5, q_ge[:, :chunk_w],
                ALU.is_le, ALU.add, r=("q_v", "q_ge"), w=("q_ge",))
            act(q_v[:, :chunk_w], qb[:, :chunk_w], AF.Sign,
                scale=c_c[:, C_NEG1:C_NEG1 + 1], r=(qk, "c_c"), w=("q_v",))
            tt(dstq[:, c * chunk_w:(c + 1) * chunk_w], q_v[:, :chunk_w],
               q_ge[:, :chunk_w], ALU.mult, r=("q_v", "q_ge"),
               w=(dstqkey + str(c),))

    quant_weight(w1T_in, NCH, HID, w1q, "w1q", 0)
    es.close()
    P.barrier()

    es = ExitStack()
    W = TB + 2
    b_xq = sb("b_xq", [128, NCH * W], BF16, es=es)
    b_h = sb("b_h", [128, NHC * W], es=es)
    b_u0 = sb("b_u0", [128, NCH * TB], es=es)
    b_i32 = sb("b_i32", [128, NCH * TB], I32, es=es)
    b_hqg = sb("b_hqg", [128, NCH * TB], BF16, es=es)
    b_ab = sb("b_ab", [128, NHC * TB], es=es)
    b_t0 = [sb(f"b_t0{i}", [128, TB], es=es) for i in range(4)]
    b_t1 = [sb(f"b_t1{i}", [128, TB], es=es) for i in range(4)]
    b_rb = sb("b_rb", [128, W], es=es)
    b_sb2 = sb("b_sb2", [128, TB], es=es)
    b_s2 = sb("b_s2", [128, 2], es=es)
    b_mx = sb("b_mx", [128, 1], es=es)

    for b in range(NB):
        for hb in range(S // TB):
            t0 = hb * TB
            lo = max(t0 - 1, 0)
            hi = min(t0 + TB + 1, S)
            width = hi - lo
            loff = 1 - (t0 - lo)
            for c in range(NCH):
                if loff > 0:
                    vmemset(b_xq[:, c * W: c * W + 1], 0.0, w=("b_xq",))
                if loff + width < W:
                    vmemset(b_xq[:, c * W + W - 1: c * W + W], 0.0, w=("b_xq",))
                ldma(b_xq[:, c * W + loff: c * W + loff + width],
                     xq_h[b, c * 128:(c + 1) * 128, lo:hi], r=("xqh",), w=("b_xq",))
            build_bcast(r1tok, "r1tok", b, lo, width, loff, b_rb, "b_rb", W, rwcol=0)
            allh = ("b_h0", "b_h1", "b_h2", "b_h3")
            for hc in range(NHC):
                bank = 5 if hc % 2 == 0 else 3
                for c in range(NCH):
                    mm(pp[bank][:, :W], w1q[:, c * HID + hc * 128: c * HID + hc * 128 + 128],
                       b_xq[:, c * W:(c + 1) * W], start=(c == 0), stop=(c == NCH - 1),
                       r=(f"w1q{c}", "b_xq"), w=(f"ps{bank}",))
                tt(b_h[:, hc * W:(hc + 1) * W], pp[bank][:, :W], b_rb[:, :W], ALU.mult,
                   r=(f"ps{bank}", "b_rb"), w=(f"b_h{hc // 8}",))
            if t0 == 0:
                vmemset(bass.AP(b_h, 0, [[NHC * W, 128], [W, NHC], [1, 1]]), 0.0,
                        r=(), w=allh)
            if hi == S:
                vmemset(bass.AP(b_h, W - 1, [[NHC * W, 128], [W, NHC], [1, 1]]), 0.0,
                        r=(), w=allh)
            def emit_taps(hc):
                par = hc % 4
                hk = f"b_h{hc // 8}"
                act(b_t0[par][:, :], b_h[:, hc * W: hc * W + TB], AF.Identity,
                    scale=s_cw0[:, hc:hc + 1], r=(hk, "s_cw0"), w=(f"b_t0{par}",))
                act(b_t1[par][:, :], b_h[:, hc * W + 1: hc * W + 1 + TB],
                    AF.Identity, scale=s_cw1[:, hc:hc + 1],
                    r=(hk, "s_cw1"), w=(f"b_t1{par}",))

            def emit_vadds(hc):
                par = hc % 4
                hk = f"b_h{hc // 8}"
                tt(b_t0[par][:, :], b_t0[par][:, :], b_t1[par][:, :], ALU.add,
                   r=(f"b_t0{par}", f"b_t1{par}"), w=(f"b_t0{par}",))
                stt(b_t0[par][:, :], b_h[:, hc * W + 2: hc * W + 2 + TB],
                    s_cw2[:, hc:hc + 1], b_t0[par][:, :], ALU.mult, ALU.add,
                    r=(hk, "s_cw2", f"b_t0{par}"), w=(f"b_t0{par}",))

            def emit_silu_abs(hc):
                par = hc % 4
                hk = f"b_h{hc // 8}"
                act(b_h[:, hc * W + 1: hc * W + 1 + TB], b_t0[par][:, :], AF.Silu,
                    bias=s_cb[:, hc:hc + 1], r=(f"b_t0{par}", "s_cb"), w=(hk,))
                act(b_ab[:, hc * TB:(hc + 1) * TB],
                    b_h[:, hc * W + 1: hc * W + 1 + TB], AF.Abs,
                    r=(hk,), w=("b_ab",))

            for hc in range(NHC + 1):
                if hc < NHC:
                    emit_taps(hc)
                if hc >= 1:
                    emit_silu_abs(hc - 1)
                if hc < NHC:
                    emit_vadds(hc)
            wfold = NHC * TB
            while wfold > TB:
                half = wfold // 2
                tt(b_ab[:, 0:half], b_ab[:, 0:half], b_ab[:, half:wfold],
                   ALU.max, r=("b_ab",), w=("b_ab",))
                wfold = half
            for sub in range(TB // 128):
                col = b * 16 + hb * (TB // 128) + sub
                transp(pp[2][:, 0:128], b_ab[:, sub * 128:(sub + 1) * 128],
                       r=("b_ab",), w=("ps2",))
                vred(b_mx[:, 0:1], pp[2][:, 0:128], ALU.max, r=("ps2",), w=("b_mx",))
                ts1(b_mx[:, 0:1], b_mx[:, 0:1], 1e-5, ALU.max, r=("b_mx",), w=("b_mx",))
                vrecip(b_s2[:, 0:1], b_mx[:, 0:1], r=("b_mx",), w=("b_s2",))
                ts1(b_s2[:, 0:1], b_s2[:, 0:1], 127.0, ALU.mult, r=("b_s2",), w=("b_s2",))
                vrecip(r2tok[:, col:col + 1], b_s2[:, 0:1], r=("b_s2",), w=("r2tok",))
                transp(pp[2][0:1, 0:128], b_s2[:, 0:1], r=("b_s2",), w=("ps2",))
                vcopy(rowt2[:1, sub * 128:(sub + 1) * 128], pp[2][0:1, 0:128],
                      r=("ps2",), w=("rowt2",))
            mm(pp[3][:, :TB], c_o1[:1, :], rowt2[:1, :TB], start=True, stop=True,
               r=("rowt2", "c_o1"), w=("ps3",))
            ts1(b_sb2[:, :TB], pp[3][:, :TB], -1.0, ALU.mult, r=("ps3",), w=("b_sb2",))
            for grp in range(4):
                hc0 = grp * 8
                tt(b_u0[:, :NCH * TB],
                   bass.AP(b_h, hc0 * W + 1, [[NHC * W, 128], [W, 8], [1, TB]]),
                   bass.AP(b_sb2, 0, [[TB, 128], [0, 8], [1, TB]]),
                   ALU.mult, r=(f"b_h{grp}", "b_sb2"), w=("b_u0",))
                act(b_i32[:, :NCH * TB], b_u0[:, :NCH * TB], AF.Identity,
                    r=("b_u0",), w=("b_i32",))
                vcopy(b_hqg[:, :NCH * TB], b_i32[:, :NCH * TB],
                      r=("b_i32",), w=("b_hqg",))
                for hh in range(8):
                    hc = hc0 + hh
                    sdma(hq_h[b, hc * 128:(hc + 1) * 128, t0:t0 + TB],
                         b_hqg[:, hh * TB:(hh + 1) * TB], r=("b_hqg",),
                         w=("hqh",) if hh % 2 == 0 else ("hqh2",),
                         q="gp" if hh % 2 == 0 else "sy")
    es.close()
    es_w1.close()
    P.barrier()

    es_w2 = ExitStack()
    w2q = sb("w2q", [128, NHC * D], BF16, es=es_w2)
    es = ExitStack()
    q_wf = sb("q_wf2", [128, D], es=es)
    q_wg = sb("q_wg2", [128, D], es=es)
    qw2 = [(q_wf, "q_wf2"), (q_wg, "q_wg2")]
    q_v = sb("q_v2", [128, D], es=es)
    q_ge = sb("q_ge2", [128, D], es=es)
    q_r = sb("q_r2", [128, 3], es=es)
    quant_weight(w2T_in, NHC, D, w2q, "w2q", 1)
    es.close()
    P.barrier()

    es = ExitStack()
    TB2 = 512
    d_hq = sb("d_hq", [128, NHC * TB2], BF16, es=es)
    d_x1 = sb("d_x1", [128, NCH * TB2], es=es)
    d_rb = sb("d_rb", [128, TB2], es=es)
    d_t2 = [sb(f"d_t{i}", [128, TB2], es=es) for i in range(2)]
    for b in range(NB):
        for hb in range(S // TB2):
            t0 = hb * TB2
            for hc in range(NHC):
                ldma(d_hq[:, hc * TB2:(hc + 1) * TB2],
                     hq_h[b, hc * 128:(hc + 1) * 128, t0:t0 + TB2],
                     r=("hqh", "hqh2"), w=("d_hq",))
            for c in range(NCH):
                ldma(d_x1[:, c * TB2:(c + 1) * TB2],
                     x1_h[b, c * 128:(c + 1) * 128, t0:t0 + TB2],
                     r=("x1h",), w=("d_x1",))
            build_bcast(r2tok, "r2tok", b, t0, TB2, 0, d_rb, "d_rb", TB2, rwcol=1)
            for oc in range(NCH):
                bank = 5 if oc % 2 == 0 else 3
                for kc in range(NHC):
                    mm(pp[bank][:, :TB2], w2q[:, kc * D + oc * 128: kc * D + oc * 128 + 128],
                       d_hq[:, kc * TB2:(kc + 1) * TB2], start=(kc == 0),
                       stop=(kc == NHC - 1), r=(f"w2q{kc}", "d_hq"), w=(f"ps{bank}",))
                dt = d_t2[oc % 2]
                dk = f"d_t{oc % 2}"
                tt(dt[:, :], pp[bank][:, :TB2], d_rb[:, :TB2], ALU.mult,
                   r=(f"ps{bank}", "d_rb"), w=(dk,))
                tt(dt[:, :], dt[:, :], d_x1[:, oc * TB2:(oc + 1) * TB2], ALU.add,
                   r=(dk, "d_x1"), w=(dk,))
                sdma(s1_h[b, oc * 128:(oc + 1) * 128, t0:t0 + TB2], dt[:, :],
                     r=(dk,), w=("s1h",))
    es.close()
    es_w2.close()
    P.barrier()

    es = ExitStack()
    t_pT = sb("t_pT", [128, 16 * 512], es=es)
    t_vT = sb("t_vT", [128, 16 * HEAD], es=es)
    t_q = sb("t_q", [HEAD, S], es=es)
    t_k = sb("t_k", [HEAD, S], es=es)
    t_v = sb("t_v", [HEAD, S], es=es)
    t_rm = sb("t_rm", [128, 1], es=es)
    t_rm2 = sb("t_rm2", [128, 1], es=es)
    t_cn = sb("t_cn", [128, 16], es=es)
    t_z = sb("t_z", [HEAD, 512], es=es)
    t_rb = sb("t_rb", [128, 512], es=es)
    t_s1 = sb("t_s1", [128, NCH * 512], es=es)
    t_o2 = [sb(f"t_o{i}", [128, 512], es=es) for i in range(2)]
    t_val = sb("t_val", [1, 4], es=es)

    QB = 512
    for b in range(NB):
        qo = 0
        ldma(t_q[0:HEAD, :], qkv_h[0, b, 0:HEAD, :], r=("qkvh",), w=("t_q",))
        ldma(t_k[0:HEAD, :], qkv_h[1, b, 0:HEAD, :], r=("qkvh",), w=("t_k",))
        ldma(t_v[0:HEAD, :], qkv_h[2, b, 0:HEAD, :], r=("qkvh",), w=("t_v",))
        for kt in range(16):
            transp(pp[2][:, 0:HEAD], t_v[0:HEAD, qo + kt * 128: qo + (kt + 1) * 128],
                   r=("t_v",), w=("ps2",), npart=HEAD)
            vcopy(t_vT[:, kt * HEAD:(kt + 1) * HEAD], pp[2][:, 0:HEAD],
                  r=("ps2",), w=("t_vT",))
        for J in range(S // QB):
            q0 = J * QB
            nkt = 4 * J + 4
            for i in range(nkt):
                pkey = f"t_pT{i}"
                bank = 5 if i % 2 == 0 else 2
                mm(pp[bank][:, :QB], t_k[0:HEAD, qo + i * 128: qo + i * 128 + 128],
                   t_q[0:HEAD, qo + q0: qo + q0 + QB], start=True, stop=True,
                   r=("t_k", "t_q"), w=(f"ps{bank}",))
                ts1(pp[bank][:, :QB], pp[bank][:, :QB], 0.125, ALU.mult,
                    r=(f"ps{bank}",), w=(f"ps{bank}",))
                for jj in range(4):
                    j = 4 * J + jj
                    cs = jj * 128
                    if j < i:
                        vmemset(t_pT[:, i * 512 + cs: i * 512 + cs + 128], 0.0,
                                w=(pkey,))
                    else:
                        if j == i:
                            tt(pp[bank][:, cs:cs + 128], pp[bank][:, cs:cs + 128],
                               c_mT[:, :], ALU.add, r=(f"ps{bank}", "c_mT"),
                               w=(f"ps{bank}",))
                        act(t_pT[:, i * 512 + cs: i * 512 + cs + 128],
                            pp[bank][:, cs:cs + 128], AF.Exp,
                            bias=c_c[:, C_M20:C_M20 + 1],
                            r=(f"ps{bank}", "c_c"), w=(pkey,))
            for i in range(nkt):
                mm(pr[0][:1, :QB], c_o128[:, :1], t_pT[:, i * 512: i * 512 + QB],
                   start=(i == 0), stop=(i == nkt - 1),
                   r=(f"t_pT{i}", "c_o128"), w=("ps6",))
            vcopy(rowt[:1, :QB], pr[0][:1, :QB], r=("ps6",), w=("rowt",))
            vrecip(rowt[:1, :QB], rowt[:1, :QB], r=("rowt",), w=("rowt",))
            mm(pp[3][:, :QB], c_o1[:1, :], rowt[:1, :QB], start=True, stop=True,
               r=("rowt", "c_o1"), w=("ps3",))
            vcopy(t_rb[:, :QB], pp[3][:, :QB], r=("ps3",), w=("t_rb",))
            for i in range(nkt):
                mm(pp[4][0:HEAD, :QB], t_vT[:, i * HEAD:(i + 1) * HEAD],
                   t_pT[:, i * 512: i * 512 + QB], start=(i == 0),
                   stop=(i == nkt - 1), r=("t_vT", f"t_pT{i}"), w=("ps4",))
            tt(t_z[0:HEAD, :QB], pp[4][0:HEAD, :QB], t_rb[0:HEAD, :QB], ALU.mult,
               r=("ps4", "t_rb"), w=("t_z",))
            for c in range(NCH):
                ldma(t_s1[:, c * QB:(c + 1) * QB],
                     s1_h[b, c * 128:(c + 1) * 128, q0:q0 + QB],
                     r=("s1h",), w=("t_s1",))
            for oc in range(NCH):
                bank = 5 if oc % 2 == 0 else 2
                ok = f"t_o{oc % 2}"
                to = t_o2[oc % 2]
                mm(pp[bank][:, :QB], s_wo[0:HEAD, oc * 128:(oc + 1) * 128],
                   t_z[0:HEAD, :QB], start=True, stop=True,
                   r=("s_wo", "t_z"), w=(f"ps{bank}",))
                act(to[:, :QB], pp[bank][:, :QB], AF.Identity, bias=s_boc[:, oc:oc + 1],
                    r=(f"ps{bank}", "s_boc"), w=(ok,))
                tt(to[:, :QB], to[:, :QB], t_s1[:, oc * QB:(oc + 1) * QB], ALU.add,
                   r=(ok, "t_s1"), w=(ok,))
                sdma(out_fm[b, oc * 128:(oc + 1) * 128, q0:q0 + QB], to[:, :QB],
                     r=(ok,), w=("OUT",))
    es.close()

    for e in P.ENGS:
        for node in P.ins[e]:
            for (we, wi) in node["waits"]:
                P.ins[we][wi]["inc"] = True
        for node in P.ins[e]:
            if node["dma"]:
                node["inc"] = True
    fams = []
    for e in P.ENGS:
        for node in P.ins[e]:
            if node["dma"] and node["fam"] not in fams:
                fams.append(node["fam"])
    fence = {}
    cnts = {}
    for e in P.ENGS:
        for i, node in enumerate(P.ins[e]):
            if node["dma"]:
                sn = "f_" + str(node["fam"])
                cnts[sn] = cnts.get(sn, 0) + 16
                fence[(e, i)] = (sn, cnts[sn])
            else:
                sn = "e_" + e
                if node["inc"]:
                    cnts[sn] = cnts.get(sn, 0) + 1
                fence[(e, i)] = (sn, cnts.get(sn, 0))
    final_store_totals = [("f_" + f, cnts.get("f_" + f, 0)) for f in
                          ("OUT", "x1h", "s1h", "hqh", "hqh2", "xn0h", "moeh", "qkvh", "xqh")
                          if "f_" + f in cnts]

    es_sem = ExitStack()
    sems = {}
    for e in P.ENGS:
        sems["e_" + e] = es_sem.enter_context(nc.semaphore(f"sem_{e}"))
    for f in fams:
        sems["f_" + str(f)] = es_sem.enter_context(nc.semaphore(f"semf_{f}"))

    with es_sem, nc.Block() as block:
        def run_engine(engobj, ename):
            maxw = {}
            for node in P.ins[ename]:
                wlist = sorted(fence[k] for k in node["waits"])
                for (sn, val) in wlist:
                    if maxw.get(sn, -1) < val:
                        engobj.wait_ge(sems[sn], val)
                        maxw[sn] = val
                ins = node["fn"](engobj)
                if node["inc"]:
                    sn = ("f_" + str(node["fam"])) if node["dma"] else ("e_" + ename)
                    ins.then_inc(sems[sn], 16 if node["dma"] else 1)

        @block.sync
        def _(sync):
            run_engine(sync, "sy")

        @block.tensor
        def _(tensor):
            run_engine(tensor, "te")

        @block.vector
        def _(vector):
            run_engine(vector, "ve")

        @block.scalar
        def _(scalar):
            run_engine(scalar, "ac")

        @block.gpsimd
        def _(gpsimd):
            run_engine(gpsimd, "gp")
            for (sn, val) in final_store_totals:
                gpsimd.wait_ge(sems[sn], val)

    es_top.close()
    return nc


def kernel(L0_attn_x, L0_attn_W1, L0_attn_b1, L0_attn_W2, L0_attn_b2,
           L0_attn_W3, L0_attn_b3, L0_attn_A,
           L0_ffn_router_indices, L0_ffn_routing_weights,
           L0_ffn_gate_up_proj, L0_ffn_gate_up_proj_bias,
           L0_ffn_down_proj, L0_ffn_down_proj_bias,
           L1_attn_wq, L1_attn_bq, L1_attn_wk, L1_attn_bk,
           L1_attn_wv, L1_attn_bv, L1_attn_wo, L1_attn_bo,
           L1_ffn_w1, L1_ffn_conv_w, L1_ffn_conv_b, L1_ffn_w2):
    f32 = np.float32

    def cc(a):
        return np.ascontiguousarray(np.asarray(a), dtype=f32)

    x = np.asarray(L0_attn_x)
    consts = np.broadcast_to(
        np.array([1e-5, -1.0, -0.5, 1.702, -20.0, 0, 0, 0], f32), (128, 8)).copy()
    ident = np.eye(128, dtype=f32)
    onesc = np.ones((1, 128), f32)
    ones128 = np.ones((128, 1), f32)
    tri = np.triu(np.ones((128, 128), f32), 1) * f32(-100000000000.0)
    mask1 = cc(tri)
    maskT = cc(tri.T)
    ridx = np.asarray(L0_ffn_router_indices)
    rw_all = np.asarray(L0_ffn_routing_weights)
    W1T = cc(np.asarray(L0_attn_W1).T)
    b1c = cc(np.asarray(L0_attn_b1).reshape(NCH, 128).T)
    w2t = cc(np.asarray(L0_attn_W2).T)
    w3t = cc(np.asarray(L0_attn_W3).T)
    b23 = cc(np.stack([np.asarray(L0_attn_b2), np.asarray(L0_attn_b3)], axis=1))
    gup = np.asarray(L0_ffn_gate_up_proj)
    perm = np.concatenate([np.arange(0, 2048, 2), np.arange(1, 2048, 2)])
    gupT = cc(gup[:, :, perm])
    gub = np.asarray(L0_ffn_gate_up_proj_bias)[:, perm]
    gubc = cc(gub.reshape(NEXP, 16, 128).transpose(0, 2, 1))
    dwnT = cc(L0_ffn_down_proj)
    dbrow = cc(np.asarray(L0_ffn_down_proj_bias).reshape(NEXP, 1, D))
    xb0 = cc(x[0].T)
    idxs_all = np.ascontiguousarray(ridx[:S].T, dtype=np.int32)
    wq = cc(L1_attn_wq)
    wk = cc(L1_attn_wk)
    wv = cc(L1_attn_wv)
    bqkv = cc(np.stack([np.asarray(L1_attn_bq), np.asarray(L1_attn_bk),
                        np.asarray(L1_attn_bv)], axis=1))
    wo = cc(L1_attn_wo)
    boc = cc(np.asarray(L1_attn_bo).reshape(NCH, 128).T)
    w1T_in = cc(np.asarray(L1_ffn_w1).T)
    w2T_in = cc(np.asarray(L1_ffn_w2).T)
    cw = np.asarray(L1_ffn_conv_w)
    cwc = cc(np.stack([cw[:, 0, j].reshape(NHC, 128).T for j in range(3)]))
    cbc = cc(np.asarray(L1_ffn_conv_b).reshape(NHC, 128).T)


    nc = build_program()
    in_maps = []
    for core in range(NCORES):
        xc = cc(x[2 * core: 2 * core + 2].transpose(0, 2, 1))
        mask_moe = np.full((128, 1), 1.0 if core == 0 else 0.0, f32)
        eid = np.full((2, 1), -float(core), f32)
        rws = cc(rw_all[:S, core].reshape(1, S))
        in_maps.append(dict(
            x_fm=xc, xb0_fm=xb0, eid=eid, consts=consts, ident=ident,
            onesc=onesc, ones128=ones128, mask1=mask1, maskT=maskT,
            mask_moe=mask_moe, idxs=idxs_all, rws=rws, W1T=W1T, b1c=b1c,
            w2t=w2t, w3t=w3t, b23=b23,
            gupT=cc(gupT[core]), gubc=cc(gubc[core]), dwnT=cc(dwnT[core]),
            dbrow=cc(dbrow[core]), wq=wq, wk=wk, wv=wv,
            bqkv=bqkv, wo=wo, boc=boc, w1T_in=w1T_in, w2T_in=w2T_in, cwc=cwc,
            cbc=cbc))
    res = run_bass_kernel_spmd(nc, in_maps, core_ids=list(range(NCORES)))
    outs = []
    for core in range(NCORES):
        o = np.asarray(res.results[core]["out_fm"])
        outs.append(o.transpose(0, 2, 1))
    return np.ascontiguousarray(np.concatenate(outs, axis=0), dtype=np.float32)
